# version 32
# baseline (speedup 1.0000x reference)
import numpy as np
import ml_dtypes
import concourse.bass as bass
import concourse.mybir as mybir
from concourse.bass_utils import run_bass_kernel_spmd

F32 = mybir.dt.float32
BF16 = mybir.dt.bfloat16
AF = mybir.ActivationFunctionType
ALU = mybir.AluOpType
AX = mybir.AxisListType
BF = ml_dtypes.bfloat16

T = 2048
D = 1024
NC = 8

ENGINES = ("pe", "ve", "act", "pool", "sp")
_ENG_ATTR = {"pe": "tensor", "ve": "vector", "act": "scalar", "pool": "gpsimd", "sp": "sync"}


class Op:
    __slots__ = ("eng", "emit", "reads", "writes", "idx", "eidx", "dma_group",
                 "inc_needed", "waits", "tag", "no_self_sync")

    def __init__(self, eng, emit, reads, writes, dma_group=None, tag="", no_self_sync=False):
        self.eng = eng
        self.emit = emit
        self.reads = reads
        self.writes = writes
        self.dma_group = dma_group
        self.inc_needed = dma_group is not None
        self.waits = {}
        self.tag = tag
        self.no_self_sync = no_self_sync


class Prog:

    def __init__(self, nc):
        self.nc = nc
        self.ops = []
        self.last_writer = {}
        self.readers = {}
        self.dma_groups = {}
        self.sems = {}
        self._sem_handles = {}

    def op(self, eng, emit, reads=(), writes=(), dma_group=None, tag="", no_self_sync=False):
        o = Op(eng, emit, list(reads), list(writes), dma_group, tag, no_self_sync)
        o.idx = len(self.ops)
        self.ops.append(o)
        if dma_group is not None:
            self.dma_groups.setdefault(dma_group, []).append(o)
        return o

    def barrier(self):
        self.op(None, None, tag="barrier")

    SELF_SYNC_WAR = False

    def _edge(self, prod, cons, war=False):
        if prod is cons:
            return
        if prod.eng == cons.eng and prod.dma_group is None and cons.dma_group is None:
            if prod.eng == "pe":
                return
            if war and not Prog.SELF_SYNC_WAR:
                return
        cons.waits[prod.idx] = prod
        prod.inc_needed = True

    def finalize(self):
        lw, rd = self.last_writer, self.readers
        barrier_preds = {}
        barrier_todo = set()
        per_eng_last = {}
        per_grp_last = {}
        for o in self.ops:
            if o.tag == "barrier":
                barrier_preds = dict(per_eng_last)
                barrier_preds.update({("g", g): p for g, p in per_grp_last.items()})
                barrier_todo = set(ENGINES)
                continue
            if o.eng in barrier_todo:
                for p in barrier_preds.values():
                    self._edge(p, o)
                barrier_todo.discard(o.eng)
            for r in o.reads:
                p = lw.get(r)
                if p is not None:
                    self._edge(p, o)
                rd.setdefault(r, []).append(o)
            for w in o.writes:
                p = lw.get(w)
                if p is not None:
                    self._edge(p, o)
                for q in rd.get(w, ()):
                    self._edge(q, o, war=True)
                lw[w] = o
                rd[w] = []
            per_eng_last[o.eng] = o
            if o.dma_group is not None:
                per_grp_last[o.dma_group] = o

        eng_ops = {e: [] for e in ENGINES}
        for o in self.ops:
            if o.tag == "barrier":
                continue
            eng_ops[o.eng].append(o)
            o.eidx = len(eng_ops[o.eng]) - 1

        def mksem(name):
            h = self.nc.alloc_semaphore(name=name)
            return h
        eng_sem = {e: mksem(f"s_{e}") for e in ENGINES}
        grp_sem = {g: mksem(f"g_{g}") for g in self.dma_groups}

        inc_count = {}
        cnt = {e: 0 for e in ENGINES}
        gcnt = {g: 0 for g in self.dma_groups}
        for o in self.ops:
            if o.tag == "barrier":
                continue
            if o.dma_group is not None:
                gcnt[o.dma_group] += 16
                inc_count[o.idx] = gcnt[o.dma_group]
            elif o.inc_needed:
                cnt[o.eng] += 1
                inc_count[o.idx] = cnt[o.eng]

        for e in ENGINES:
            high = {}
            for o in eng_ops[e]:
                wlist = []
                for p in o.waits.values():
                    if p.dma_group is not None:
                        sem, val = grp_sem[p.dma_group], inc_count[p.idx]
                    else:
                        if not p.inc_needed:
                            continue
                        sem, val = eng_sem[p.eng], inc_count[p.idx]
                    if high.get(sem, -1) < val:
                        high[sem] = val
                        wlist.append((sem, val))
                o.waits = wlist

        with self.nc.Block() as block:
            for e in ENGINES:
                ops_e = eng_ops[e]
                if not ops_e:
                    continue
                def mk(fn_ops, e=e):
                    def run(eng):
                        for o in fn_ops:
                            for (sem, val) in o.waits:
                                eng.wait_ge(sem, val)
                            inst = o.emit(eng)
                            if o.inc_needed:
                                if o.dma_group is not None:
                                    inst.then_inc(grp_sem[o.dma_group], 16)
                                else:
                                    inst.then_inc(eng_sem[e], 1)
                    return run
                getattr(block, _ENG_ATTR[e])(mk(ops_e))
        return



class Tile:
    __slots__ = ("h", "name")

    def __init__(self, h, name):
        self.h = h
        self.name = name

    def __getitem__(self, key):
        return self.h[key]

    def r(self, key=0):
        return (self.name, key)


class Ctx:
    def __init__(self, nc, P, dram, consts_np):
        self.nc = nc
        self.P = P
        self.dram = dram
        self.uid = 0
        self.scopes = []
        self.pA = nc.alloc_psum_tensor("pA", [128, 2048], F32)
        self.pB = nc.alloc_psum_tensor("pB", [128, 2048], F32)
        self.X = Tile(nc.alloc_sbuf_tensor("X", [128, 8 * T], F32), "X")
        ctx = self

        class _Consts:
            def __init__(self):
                self.loaded = {}

            def __getitem__(self, k):
                if k not in self.loaded:
                    v = consts_np[k]
                    h = nc.inline_tensor(np.ascontiguousarray(v), f"c_{k}")
                    shp = list(v.shape) if v.ndim == 2 else [1, v.shape[0]]
                    t = nc.alloc_sbuf_tensor(f"ct_{k}", shp, BF16 if v.dtype == BF else F32,
                                             side="right")
                    self.loaded[k] = t
                    ctx.dma("sp", t[:], h.ap(), reads=[], writes=[("const", k)], group="cload")
                return self.loaded[k]

        self.const = _Consts()
        self.consts_np = consts_np

    def const_scoped(self, k):
        v = self.consts_np[k]
        h = self.nc.inline_tensor(np.ascontiguousarray(v), self.nm(f"cs_{k}"))
        shp = list(v.shape) if v.ndim == 2 else [1, v.shape[0]]
        t = self.sb(shp, BF16 if v.dtype == BF else F32, f"cst_{k}")
        self.dma("sp", t[:], h.ap(), reads=[], writes=[t.r()], group="cload")
        return t

    def nm(self, s):
        self.uid += 1
        return f"{s}{self.uid}"

    def sb(self, shape, dtype, name=None):
        nm = self.nm(name or "t")
        if self.scopes:
            h = self.scopes[-1].enter_context(self.nc.sbuf_tensor(nm, list(shape), dtype))
        else:
            h = self.nc.alloc_sbuf_tensor(nm, list(shape), dtype)
        return Tile(h, nm)

    from contextlib import contextmanager as _cm

    @_cm
    def scoped(self):
        from contextlib import ExitStack
        st = ExitStack()
        self.scopes.append(st)
        try:
            yield
        finally:
            self.scopes.pop()
            st.close()

    def mm(self, out, lhsT, rhs, reads, wregion, start=True, stop=True, tile_position=None, tag="mm"):
        def e(eng, out=out, lhsT=lhsT, rhs=rhs, start=start, stop=stop, tp=tile_position):
            return eng.matmul(out, lhsT, rhs, start=start, stop=stop, tile_position=tp)
        self.P.op("pe", e, reads=reads, writes=[wregion], tag=tag)

    def mm_acc(self, out, pairs, reads, wregion, tag="mma", tile_position=None):
        n = len(pairs)
        for i, (l, r) in enumerate(pairs):
            self.mm(out, l, r, reads, wregion, start=(i == 0), stop=(i == n - 1),
                    tag=tag, tile_position=tile_position)

    def transp(self, out, src, reads, wregion, dtype_bf=True, tag="tr"):
        ident = self.const["ident"] if dtype_bf else self.const["identf"]
        self.mm(out, src, ident[:], reads, wregion, tag=tag)

    def ve(self, emit, reads, writes, tag="ve"):
        self.P.op("ve", emit, reads, writes, tag=tag)

    def act(self, emit, reads, writes, tag="act"):
        self.P.op("act", emit, reads, writes, tag=tag)

    def pool(self, emit, reads, writes, tag="pool"):
        self.P.op("pool", emit, reads, writes, tag=tag)

    def cp(self, eng, out, in_, reads, writes, tag="cp"):
        if eng == "act":
            self.P.op(eng, lambda e, o=out, i=in_: e.copy(o, i), reads, writes, tag=tag)
        else:
            self.P.op(eng, lambda e, o=out, i=in_: e.tensor_copy(o, i), reads, writes, tag=tag)

    def tt(self, out, in0, in1, op, reads, writes, eng="ve", tag="tt"):
        self.P.op(eng, lambda e, o=out, a=in0, b=in1, p=op: e.tensor_tensor(o, a, b, p),
                  reads, writes, tag=tag)

    def ts(self, out, in0, s1, s2, op0, op1, reads, writes, eng="ve", tag="ts"):
        if s2 is None:
            self.P.op(eng, lambda e, o=out, a=in0, x=s1, p=op0: e.tensor_single_scalar(o, a, x, p),
                      reads, writes, tag=tag)
        else:
            self.P.op(eng, lambda e, o=out, a=in0, x=s1, y=s2, p=op0, q=op1:
                      e.tensor_scalar(o, a, x, y, p, q), reads, writes, tag=tag)

    def stt(self, out, in0, scalar, in1, op0, op1, reads, writes, eng="ve", tag="stt"):
        self.P.op(eng, lambda e, o=out, a=in0, s=scalar, b=in1, p=op0, q=op1:
                  e.scalar_tensor_tensor(o, a, s, b, p, q), reads, writes, tag=tag)

    def actv(self, out, in_, func, reads, writes, bias=0.0, scale=1.0, accum=None, tag="actv"):
        self.P.op("act", lambda e, o=out, i=in_, f=func, b=bias, s=scale, ac=accum:
                  e.activation(o, i, f, bias=b, scale=s, accum_out=ac), reads, writes, tag=tag)

    def red(self, out, in_, op, reads, writes, negate=False, eng="ve", tag="red"):
        self.P.op(eng, lambda e, o=out, i=in_, p=op, n=negate:
                  e.tensor_reduce(o, i, AX.X, p, negate=n), reads, writes, tag=tag)

    def recip(self, out, in_, reads, writes, tag="recip"):
        self.P.op("ve", lambda e, o=out, i=in_: e.reciprocal(o, i), reads, writes, tag=tag)

    def memset(self, ap, val, writes, eng="pool", tag="memset"):
        self.P.op(eng, lambda e, a=ap, v=val: e.memset(a, v), [], writes, tag=tag)

    def dma(self, queue, out, in_, reads, writes, group, tag="dma"):
        self.P.op(queue, lambda e, o=out, i=in_: e.dma_start(o, i),
                  reads, writes, dma_group=group, tag=tag)

    DEBUG = False

    def dbg(self, name, ap, reads):
        if not Ctx.DEBUG:
            return
        shp = list(ap.shape)
        t = self.nc.dram_tensor(f"dbg_{name}", shp, ap.dtype, kind="ExternalOutput").ap()
        self.dma("sp", t, ap, reads=reads, writes=[("dram", f"dbg_{name}")], group="xstore")



def ln_factors(ctx, src, src_regions, nchunks=8, dim=None, want_rms2=False,
               eps_ln=1e-5, eps_rms=1e-6, src_is_bf=False, rms_only=False,
               eps_rms_only=None):
    dim = dim if dim is not None else nchunks * 128
    ones = ctx.const["ones128"]
    onesr = ("const", "ones128")
    out = {}
    B = ctx.sb([128, T], F32, "lnB")
    out["B"] = B
    if not rms_only:
        A = ctx.sb([128, T], F32, "lnA")
        out["A"] = A
    if want_rms2:
        r = ctx.sb([128, T], F32, "lnr")
        out["B2"] = r
    with ctx.scoped():
        sq2 = [ctx.sb([128, T], BF16, "lnsq") for _ in range(2)]
        if not src_is_bf and not rms_only:
            xb2 = [ctx.sb([128, T], BF16, "lnxb") for _ in range(2)]
        for c in range(nchunks):
            sl = slice(c * T, (c + 1) * T)
            s = c % 2
            sq = sq2[s]
            ctx.actv(sq[:], src[:, sl], AF.Square, [src_regions[c]], [sq.r()])
            if not rms_only:
                if src_is_bf:
                    xbt, xread = src, src_regions[c]
                    xsl = sl
                else:
                    xbt, xread = xb2[s], xb2[s].r()
                    xsl = slice(0, T)
                    ctx.cp("ve", xbt[:], src[:, sl], [src_regions[c]], [xread])
            for nb in range(4):
                osl = slice(nb * 512, (nb + 1) * 512)
                if not rms_only:
                    csl = slice(xsl.start + nb * 512, xsl.start + (nb + 1) * 512)
                    ctx.mm(ctx.pA[:, osl], ones[:], xbt[:, csl], [onesr, xread], ("pA", nb),
                           start=(c == 0), stop=(c == nchunks - 1), tag="lnS1")
                ctx.mm(ctx.pB[:, osl], ones[:], sq[:, nb * 512:(nb + 1) * 512], [onesr, sq.r()],
                       ("pB", nb), start=(c == 0), stop=(c == nchunks - 1), tag="lnS2")
        tmp = ctx.sb([128, T], F32, "lntmp")
        pAr = [("pA", nb) for nb in range(4)]
        pBr = [("pB", nb) for nb in range(4)]
        inv_d = 1.0 / dim
        if rms_only:
            ctx.ts(tmp[:], ctx.pB[:, 0:T], inv_d, eps_rms_only, ALU.mult, ALU.add, pBr, [tmp.r()])
        else:
            ctx.ts(A[:], ctx.pA[:, 0:T], inv_d, None, ALU.mult, None, pAr, [A.r()])
            ctx.tt(tmp[:], A[:], A[:], ALU.mult, [A.r()], [tmp.r()])
            ctx.ts(B[:], ctx.pB[:, 0:T], inv_d, eps_ln, ALU.mult, ALU.add, pBr, [B.r()])
            ctx.tt(tmp[:], B[:], tmp[:], ALU.subtract, [B.r(), tmp.r()], [tmp.r()])
        if want_rms2:
            ctx.recip(r[:], tmp[:], [tmp.r()], [r.r()])
            ctx.ts(r[:], r[:], -eps_ln, 1.0 + eps_rms, ALU.mult, ALU.add, [r.r()], [r.r()])
            ctx.actv(r[:], r[:], AF.Ln, [r.r()], [r.r()])
            ctx.actv(r[:], r[:], AF.Exp, [r.r()], [r.r()], scale=-0.5)
        ctx.actv(B[:], tmp[:], AF.Ln, [tmp.r()], [B.r()])
        ctx.actv(B[:], B[:], AF.Exp, [B.r()], [B.r()], scale=-0.5)
    ctx.P.barrier()
    return out



def load_pc(ctx, name, C, dtype=F32, queue="sp"):
    t = ctx.sb([128, C], dtype, f"pc_{name}")
    ctx.dma(queue if dtype == F32 else "pool", t[:], ctx.dram[name],
            reads=[], writes=[t.r()], group="small")
    return t


def stage_moe(ctx):
    d = ctx.dram
    X = ctx.X
    Xr = [X.r(c) for c in range(8)]
    with ctx.scoped():
        xn = ctx.sb([128, 8 * T], BF16, "xn")
        with ctx.scoped():
            f = ln_factors(ctx, X, Xr, want_rms2=True)
            A, B, B2 = f["A"], f["B"], f["B2"]
            Bc = ctx.sb([128, T], F32, "Bc")
            ctx.tt(Bc[:], B[:], B2[:], ALU.mult, [B.r(), B2.r()], [Bc.r()])
            scl = load_pc(ctx, "moe_scale_pc", 8)
            t1 = ctx.sb([128, T], F32, "t1")
            u1 = ctx.sb([128, T], F32, "u1")
            for c in range(8):
                sl = slice(c * T, (c + 1) * T)
                ctx.tt(t1[:], X[:, sl], A[:], ALU.subtract, [Xr[c], A.r()], [t1.r()])
                ctx.stt(xn[:, sl], t1[:], scl[:, c:c + 1], Bc[:], ALU.mult, ALU.mult,
                        [t1.r(), scl.r(), Bc.r()], [xn.r(c)])
                ctx.stt(u1[:], t1[:], 0.5, B[:], ALU.mult, ALU.mult, [t1.r(), B.r()], [u1.r()])
                ctx.tt(X[:, sl], X[:, sl], u1[:], ALU.add, [Xr[c], u1.r()], [Xr[c]])
        ctx.P.barrier()
        xnr = [xn.r(c) for c in range(8)]

        routerT = ctx.sb([128, 8 * 8], BF16, "routerT")
        ctx.dma("pool", routerT[:].rearrange("p (c n) -> p c n", n=8),
                d["moe_routerT"].rearrange("(c p) n -> p c n", p=128),
                reads=[], writes=[routerT.r()], group="small2")
        for tc in range(16):
            ctx.mm_acc(ctx.pA[:, tc * 8:(tc + 1) * 8],
                       [(xn[:, c * T + tc * 128: c * T + (tc + 1) * 128], routerT[:, c * 8:(c + 1) * 8])
                        for c in range(8)],
                       reads=xnr + [routerT.r()], wregion=("pA", 0), tag="router")
        s8 = ctx.sb([128, 128], F32, "s8")
        ctx.cp("ve", s8[:], ctx.pA[:, 0:128], [("pA", 0)], [s8.r()])

        w8 = ctx.sb([128, 128], F32, "w8")
        m1 = ctx.sb([128, 16], F32, "m1")
        m2 = ctx.sb([128, 16], F32, "m2")
        oh = ctx.sb([128, 128], F32, "oh")
        msk = ctx.sb([128, 128], F32, "msk")
        wgt = ctx.sb([128, 16 * 2], F32, "wgt")
        for g in range(16):
            gs = slice(g * 8, (g + 1) * 8)
            gc = slice(g, g + 1)
            ctx.red(m1[:, gc], s8[:, gs], ALU.max, [s8.r()], [m1.r(g)])
            ctx.ts(oh[:, gs], s8[:, gs], m1[:, gc], None, ALU.is_ge, None,
                   [s8.r(), m1.r(g)], [oh.r(g)])
            ctx.stt(msk[:, gs], oh[:, gs], -1e30, s8[:, gs], ALU.mult, ALU.add,
                    [oh.r(g), s8.r()], [msk.r(g)])
            ctx.red(m2[:, gc], msk[:, gs], ALU.max, [msk.r(g)], [m2.r(g)])
        ctx.tt(wgt[:, 0:16], m2[:], m1[:], ALU.subtract,
               [m2.r(g) for g in range(16)] + [m1.r(g) for g in range(16)], [wgt.r("d")])
        ctx.actv(wgt[:, 0:16], wgt[:, 0:16], AF.Sigmoid, [wgt.r("d")], [wgt.r("d")])
        ctx.ts(wgt[:, 0:16], wgt[:, 0:16], 0.5, None, ALU.mult, None, [wgt.r("d")], [wgt.r("d")])
        ctx.ts(wgt[:, 16:32], wgt[:, 0:16], -1.0, 0.5, ALU.mult, ALU.add,
               [wgt.r("d")], [wgt.r("w1")])
        for g in range(16):
            gs = slice(g * 8, (g + 1) * 8)
            ctx.ts(msk[:, gs], msk[:, gs], m2[:, g:g + 1], None, ALU.is_ge, None,
                   [msk.r(g), m2.r(g)], [msk.r(g)])
            ctx.ts(oh[:, gs], oh[:, gs], wgt[:, 16 + g:17 + g], None, ALU.mult, None,
                   [oh.r(g), wgt.r("w1")], [oh.r(g)])
            ctx.stt(w8[:, gs], msk[:, gs], wgt[:, g:g + 1], oh[:, gs], ALU.mult, ALU.add,
                    [msk.r(g), wgt.r("d"), oh.r(g)], [w8.r(g)])
        w8r = [w8.r(g) for g in range(16)]
        ctx.mm(ctx.pB[:, 0:128], w8[:], ctx.const["identf"][:], w8r + [("const", "identf")],
               ("pB", 0), tag="w8T")
        w8T = ctx.sb([128, 128], F32, "w8T")
        ctx.cp("ve", w8T[:], ctx.pB[:, 0:128], [("pB", 0)], [w8T.r()])
        scname = ctx.nm("w8scr")
        scr = ctx.nc.dram_tensor(scname, [128, 128], F32, kind="Internal").ap()
        ctx.dma("sp", scr, w8T[:], reads=[w8T.r()], writes=[("dram", scname)], group="w8io")

        wb = ctx.sb([128, 16 * 128], F32, "wb")
        hbuf = ctx.sb([128, 16 * T], BF16, "hbuf")
        sgt = ctx.sb([128, T], BF16, "sgt")
        uwt = ctx.sb([128, T], BF16, "uwt")
        wu = [ctx.sb([128, 2 * 1024], BF16, "wu") for _ in range(4)]
        wd = [ctx.sb([128, 16 * 128], BF16, "wd") for _ in range(2)]
        for e in range(8):
            win = scr.rearrange("(g e) q -> e g q", e=8)[e]
            win = win.unsqueeze(0).broadcast_to([128, 16, 128])
            ctx.dma("sp", wb[:].rearrange("p (g q) -> p g q", g=16), win,
                    reads=[("dram", scname)], writes=[wb.r()], group="wbload")
            for i in range(16):
                s = i % 4
                m_u, m_g = i * 128, (16 + i) * 128
                ctx.dma("pool", wu[s][:, 0:1024].rearrange("p (k m) -> p k m", m=128),
                        d["moe_upT"][e, i],
                        reads=[], writes=[wu[s].r()], group=f"wu{s}")
                ctx.dma("pool", wu[s][:, 1024:2048].rearrange("p (k m) -> p k m", m=128),
                        d["moe_upT"][e, 16 + i],
                        reads=[], writes=[wu[s].r()], group=f"wu{s}")
                for nb in range(4):
                    nsl = slice(nb * 512, (nb + 1) * 512)
                    ctx.mm_acc(ctx.pA[:, nsl],
                               [(wu[s][:, kc * 128:(kc + 1) * 128],
                                 xn[:, kc * T + nb * 512: kc * T + (nb + 1) * 512]) for kc in range(8)],
                               reads=[wu[s].r()] + xnr, wregion=("pA", nb), tag="up_u")
                    ctx.mm_acc(ctx.pB[:, nsl],
                               [(wu[s][:, 1024 + kc * 128: 1024 + (kc + 1) * 128],
                                 xn[:, kc * T + nb * 512: kc * T + (nb + 1) * 512]) for kc in range(8)],
                               reads=[wu[s].r()] + xnr, wregion=("pB", nb), tag="up_g")
                pAr = [("pA", nb) for nb in range(4)]
                pBr = [("pB", nb) for nb in range(4)]
                ctx.actv(sgt[:], ctx.pB[:, 0:T], AF.Silu, pBr, [sgt.r()])
                ctx.tt(uwt[:], ctx.pA[:, 0:T], wb[:, 0:T], ALU.mult, pAr + [wb.r()], [uwt.r()])
                ctx.tt(hbuf[:, i * T:(i + 1) * T], uwt[:], sgt[:], ALU.mult,
                       [uwt.r(), sgt.r()], [hbuf.r(i)])
            for do in range(8):
                s = do % 2
                ctx.dma("pool", wd[s][:].rearrange("p (k m) -> p k m", m=128),
                        d["moe_downT"][e, do],
                        reads=[], writes=[wd[s].r()], group=f"wd{s}")
                for nb in range(4):
                    nsl = slice(nb * 512, (nb + 1) * 512)
                    ctx.mm_acc(ctx.pA[:, nsl],
                               [(wd[s][:, kc * 128:(kc + 1) * 128],
                                 hbuf[:, kc * T + nb * 512: kc * T + (nb + 1) * 512]) for kc in range(16)],
                               reads=[wd[s].r()] + [hbuf.r(i) for i in range(16)],
                               wregion=("pA", nb), tag="down")
                    xsl = slice(do * T + nb * 512, do * T + (nb + 1) * 512)
                    ctx.tt(X[:, xsl], X[:, xsl], ctx.pA[:, nsl], ALU.add,
                           [X.r(do), ("pA", nb)], [X.r(do)])


stage_moe.needs = ["moe_scale_pc", "moe_routerT", "moe_upT", "moe_downT"]



def stage_l1attn(ctx):
    d = ctx.dram
    X = ctx.X
    Xr = [X.r(c) for c in range(8)]
    with ctx.scoped():
        xb = ctx.sb([128, 8 * T], BF16, "xb")
        with ctx.scoped():
            f = ln_factors(ctx, X, Xr)
            A, B = f["A"], f["B"]
            t1 = ctx.sb([128, T], F32, "t1")
            t1b = ctx.sb([128, T], F32, "t1b")
            u1 = ctx.sb([128, T], F32, "u1")
            u1b = ctx.sb([128, T], F32, "u1b")
            for c in range(8):
                sl = slice(c * T, (c + 1) * T)
                eng = "ve"
                tl = t1 if c % 2 == 0 else t1b
                ul = u1 if c % 2 == 0 else u1b
                ctx.tt(tl[:], X[:, sl], A[:], ALU.subtract, [Xr[c], A.r()], [tl.r()], eng=eng)
                ctx.tt(xb[:, sl], tl[:], B[:], ALU.mult, [tl.r(), B.r()], [xb.r(c)], eng=eng)
                ctx.tt(ul[:], tl[:], B[:], ALU.mult, [tl.r(), B.r()], [ul.r()], eng=eng)
                ctx.tt(X[:, sl], X[:, sl], ul[:], ALU.add, [Xr[c], ul.r()], [Xr[c]], eng=eng)
        ctx.P.barrier()
        xbr = [xb.r(c) for c in range(8)]

        bq = load_pc(ctx, "a1_bqkv_pc", 24)
        bo = load_pc(ctx, "a1_bo_pc", 8)
        bvr = ctx.sb([1, 1024], F32, "bvrow")
        ctx.dma("sp", bvr[:], d["a1_bv_row"], reads=[], writes=[bvr.r()], group="small")
        bvb = ctx.sb([128, 1024], F32, "bvb")
        for half in range(2):
            ctx.mm(ctx.pB[:, half * 512:(half + 1) * 512], ctx.const["onescol_f"][:],
                   bvr[:, half * 512:(half + 1) * 512], [("const", "onescol_f"), bvr.r()],
                   ("pB", half), tag="bvb")
            ctx.cp("ve", bvb[:, half * 512:(half + 1) * 512], ctx.pB[:, half * 512:(half + 1) * 512],
                   [("pB", half)], [bvb.r()])

        attn_out = ctx.sb([128, 8 * T], BF16, "aout")
        qt = ctx.sb([128, T], BF16, "qt")
        kt = ctx.sb([128, T], BF16, "kt")
        vt = ctx.sb([128, T], BF16, "vt")
        NRING = 3
        NRT = 2
        pt = [ctx.sb([128, T], BF16, "pt") for _ in range(NRING)]
        ptT = [ctx.sb([128, T], BF16, "ptT") for _ in range(NRT)]
        mx = [ctx.sb([128, 1], F32, "mx") for _ in range(NRING)]
        den = [ctx.sb([128, 1], F32, "den") for _ in range(NRING)]
        wsl = [ctx.sb([128, 3 * 1024], BF16, "wqkv") for _ in range(2)]
        for hc in range(8):
            s = hc % 2
            for part in range(3):
                ctx.dma("pool", wsl[s][:, part * 1024:(part + 1) * 1024].rearrange("p (k m) -> p k m", m=128),
                        d["a1_qkvT"][part * 8 + hc],
                        reads=[], writes=[wsl[s].r()], group=f"qkv{s}")
            wr = [wsl[s].r()]
            for nb in range(4):
                nsl = slice(nb * 512, (nb + 1) * 512)
                ctx.mm_acc(ctx.pA[:, nsl],
                           [(wsl[s][:, kc * 128:(kc + 1) * 128],
                             xb[:, kc * T + nb * 512: kc * T + (nb + 1) * 512]) for kc in range(8)],
                           reads=wr + xbr, wregion=("pA", nb), tag="q")
                ctx.mm_acc(ctx.pB[:, nsl],
                           [(wsl[s][:, 1024 + kc * 128: 1024 + (kc + 1) * 128],
                             xb[:, kc * T + nb * 512: kc * T + (nb + 1) * 512]) for kc in range(8)],
                           reads=wr + xbr, wregion=("pB", nb), tag="k")
            pAr = [("pA", nb) for nb in range(4)]
            pBr = [("pB", nb) for nb in range(4)]
            ctx.ts(qt[:], ctx.pA[:, 0:T], bq[:, hc:hc + 1], 1.0 / 32.0, ALU.add, ALU.mult,
                   pAr + [bq.r()], [qt.r()])
            ctx.ts(kt[:], ctx.pB[:, 0:T], bq[:, 8 + hc:9 + hc], None, ALU.add, None,
                   pBr + [bq.r()], [kt.r()])
            for tc in range(16):
                bank = tc % 4
                psl = slice(bank * 512, bank * 512 + 128)
                ctx.mm_acc(ctx.pA[:, psl],
                           [(xb[:, kc * T + tc * 128: kc * T + (tc + 1) * 128],
                             wsl[s][:, 2048 + kc * 128: 2048 + (kc + 1) * 128]) for kc in range(8)],
                           reads=wr + xbr, wregion=("pA", bank), tag="v")
                ctx.tt(vt[:, tc * 128:(tc + 1) * 128], ctx.pA[:, psl],
                       bvb[:, hc * 128:(hc + 1) * 128], ALU.add,
                       [("pA", bank), bvb.r()], [vt.r(tc)], tag="vevac")
            vtr = [vt.r(tc) for tc in range(16)]
            av_sl = slice(3 * 512, 3 * 512 + 128)
            groups = [(qc, j) for qc in range(16) for j in range(2)]

            def front(g):
                qc, j = groups[g]
                b0 = 64 * j
                rp = g % NRING
                for nb in range(4):
                    ctx.mm(ctx.pB[:, nb * 512:(nb + 1) * 512],
                           qt[b0:b0 + 64, qc * 128:(qc + 1) * 128],
                           kt[b0:b0 + 64, nb * 512:(nb + 1) * 512],
                           [qt.r(), kt.r()], ("pB", nb), tag="sc")
                ctx.red(mx[rp][:], ctx.pB[:, 0:T], ALU.max, pBr, [mx[rp].r()], negate=True)
                ctx.actv(pt[rp][:], ctx.pB[:, 0:T], AF.Exp, pBr + [mx[rp].r()],
                         [pt[rp].r(), den[rp].r()], bias=mx[rp][:], accum=den[rp][:])
                ctx.recip(den[rp][:], den[rp][:], [den[rp].r()], [den[rp].r()])
                ctx.ts(pt[rp][:], pt[rp][:], den[rp][:], None, ALU.mult, None,
                       [pt[rp].r(), den[rp].r()], [pt[rp].r()])

            def back_tr(g):
                rp = g % NRING
                rt = g % NRT
                for batch in range(4):
                    bank = (g * 4 + batch) % 3
                    for jc4 in range(4):
                        jc = batch * 4 + jc4
                        psl = slice(bank * 512 + jc4 * 128, bank * 512 + (jc4 + 1) * 128)
                        ctx.mm(ctx.pA[:, psl], pt[rp][:, jc * 128:(jc + 1) * 128],
                               ctx.const["ident"][:], [pt[rp].r(), ("const", "ident")],
                               ("pA", bank), tag="ptT")
                    ctx.cp("act" if batch % 2 == 0 else "ve",
                           ptT[rt][:, batch * 512:(batch + 1) * 512],
                           ctx.pA[:, bank * 512:(bank + 1) * 512],
                           [("pA", bank)], [ptT[rt].r(batch)])

            def back_av(g):
                qc, j = groups[g]
                b0 = 64 * j
                rt = g % NRT
                ctx.mm_acc(ctx.pA[b0:b0 + 64, av_sl],
                           [(vt[:, jc * 128 + b0: jc * 128 + b0 + 64],
                             ptT[rt][:, jc * 128:(jc + 1) * 128]) for jc in range(16)],
                           reads=vtr + [ptT[rt].r(b) for b in range(4)],
                           wregion=("pA", 3), tag="av", tile_position=(0, b0))
                if j == 1:
                    ctx.cp("act", attn_out[:, hc * T + qc * 128: hc * T + (qc + 1) * 128],
                           ctx.pA[:, av_sl], [("pA", 3)], [attn_out.r(hc)])

            NG = len(groups)
            for gi in range(NG + 3):
                if gi >= 3:
                    back_av(gi - 3)
                if 2 <= gi < NG + 2:
                    back_tr(gi - 2)
                if gi < NG:
                    front(gi)
        wo = [ctx.sb([128, 1024], BF16, "wo") for _ in range(2)]
        aor = [attn_out.r(c) for c in range(8)]
        for oc in range(8):
            s = oc % 2
            ps = ctx.pA if oc % 2 == 0 else ctx.pB
            pn = "pA" if oc % 2 == 0 else "pB"
            ctx.dma("pool", wo[s][:].rearrange("p (k m) -> p k m", m=128),
                    d["a1_woT"][oc],
                    reads=[], writes=[wo[s].r()], group=f"wo{s}")
            for nb in range(4):
                nsl = slice(nb * 512, (nb + 1) * 512)
                ctx.mm_acc(ps[:, nsl],
                           [(wo[s][:, kc * 128:(kc + 1) * 128],
                             attn_out[:, kc * T + nb * 512: kc * T + (nb + 1) * 512]) for kc in range(8)],
                           reads=[wo[s].r()] + aor, wregion=(pn, nb), tag="woP")
                xsl = slice(oc * T + nb * 512, oc * T + (nb + 1) * 512)
                ctx.stt(X[:, xsl], ps[:, nsl], bo[:, oc:oc + 1], X[:, xsl],
                        ALU.add, ALU.add, [(pn, nb), bo.r(), X.r(oc)], [X.r(oc)])


stage_l1attn.needs = ["a1_qkvT", "a1_bqkv_pc", "a1_bv_row", "a1_woT", "a1_bo_pc"]



L0_TP = (2050, 2060)
L0_L = 2176


def _l0_branch(ctx, br, xin, xinr, pout, poutr):
    d = ctx.dram
    L = L0_L
    nL = L // 128
    with ctx.scoped():
        xpad = ctx.sb([128, 2 * L], BF16, "xpad")
        for cc in range(2):
            ctx.cp("ve", xpad[:, cc * L: cc * L + T], xin[:, (2 * br + cc) * T:(2 * br + cc + 1) * T],
                   [xinr[2 * br + cc]], [xpad.r(cc)])
            ctx.memset(xpad[:, cc * L + T: (cc + 1) * L], 0.0, [xpad.r(cc)], eng="pool")
        xpr = [xpad.r(0), xpad.r(1)]
        qb = ctx.sb([128, 2 * L], BF16, "qb")
        kb = ctx.sb([128, 2 * L], BF16, "kb")
        vt = ctx.sb([128, nL * 256], BF16, "vt")
        qkvb = load_pc(ctx, "a0_qkvb_pc", 12)
        wq = ctx.sb([128, 512], BF16, "wq")
        bvr = ctx.sb([1, 256], F32, "bvr")
        ctx.dma("sp", bvr[:], d["a0_bv"][br], reads=[], writes=[bvr.r()], group="small")
        bvb = ctx.sb([128, 256], F32, "bvb")
        ctx.mm(ctx.pB[:, 0:256], ctx.const["onescol_f"][:], bvr[:],
               [("const", "onescol_f"), bvr.r()], ("pB", 0), tag="bvb0")
        ctx.cp("ve", bvb[:], ctx.pB[:, 0:256], [("pB", 0)], [bvb.r()])

        nb5 = [(0, 512), (512, 512), (1024, 512), (1536, 512), (2048, 128)]
        for part, (dst, dstr) in enumerate(((qb, "q"), (kb, "k"))):
            ctx.dma("pool", wq[:].rearrange("p (k m) -> p k m", m=256),
                    d["a0_qkv_w"][br][:, part * 256:(part + 1) * 256]
                    .rearrange("(k p) m -> p k m", p=128),
                    reads=[], writes=[wq.r()], group="a0w")
            for hg in range(2):
                for (n0, nn) in nb5:
                    po = n0 if n0 < 2048 else 0
                    ctx.mm_acc(ctx.pA[:, po:po + nn],
                               [(wq[:, kc * 256 + hg * 128: kc * 256 + hg * 128 + 128],
                                 xpad[:, cc * L + n0: cc * L + n0 + nn])
                                for kc, cc in ((0, 0), (1, 1))],
                               reads=[wq.r()] + xpr, wregion=("pA", po // 512),
                               tag="a0qk")
                    bcol = qkvb[:, br * 6 + part * 2 + hg: br * 6 + part * 2 + hg + 1]
                    ctx.ts(dst[:, hg * L + n0: hg * L + n0 + nn], ctx.pA[:, po:po + nn],
                           bcol, (1.0 / np.sqrt(32.0)) if part == 0 else None,
                           ALU.add, ALU.mult if part == 0 else None,
                           [("pA", po // 512), qkvb.r()], [dst.r((dstr, hg))])
        ctx.dma("pool", wq[:].rearrange("p (k m) -> p k m", m=256),
                d["a0_qkv_w"][br][:, 512:768].rearrange("(k p) m -> p k m", p=128),
                reads=[], writes=[wq.r()], group="a0w")
        for tc in range(nL):
            bank = tc % 4
            psl = slice(bank * 512, bank * 512 + 256)
            ctx.mm_acc(ctx.pA[:, psl],
                       [(xpad[:, kc2 * L + tc * 128: kc2 * L + tc * 128 + 128],
                         wq[:, kc2 * 256: kc2 * 256 + 256]) for kc2 in range(2)],
                       reads=[wq.r()] + xpr, wregion=("pA", bank), tag="a0v")
            ctx.tt(vt[:, tc * 256:(tc + 1) * 256], ctx.pA[:, psl], bvb[:], ALU.add,
                   [("pA", bank), bvb.r()], [vt.r(tc)])
        vtr = [vt.r(tc) for tc in range(nL)]
        qbr = [qb.r(("q", 0)), qb.r(("q", 1))]
        kbr = [kb.r(("k", 0)), kb.r(("k", 1))]

        NR0 = 3
        NRT0 = 2
        bias_sb = [ctx.sb([128, 1024], BF16, "a0bias") for _ in range(3)]
        po = [ctx.sb([128, 1024], BF16, "po") for _ in range(NR0)]
        poT = [ctx.sb([128, 1024], BF16, "poT") for _ in range(NRT0)]
        mx = [ctx.sb([128, 1], F32, "mx") for _ in range(NR0)]
        den = [ctx.sb([128, 1], F32, "den") for _ in range(NR0)]
        bout = ctx.sb([128, 2 * T], BF16, "bout")
        av_sl = slice(3 * 512, 3 * 512 + 128)
        amask_tiles = {}
        groups = [(qc, h) for qc in range(16) for h in range(8)]

        def meta(g):
            qc, h = groups[g]
            s0 = min(max((qc - 1) * 128, 0), L - 512)
            s1 = 1664 if qc != 15 else 0
            nband = 2 if qc in (0, 15) else 1
            return qc, h, s0, s1, nband

        def front(g):
            qc, h, s0, s1, nband = meta(g)
            NW = nband * 512
            rp = g % NR0
            sb_ = g % 3
            b0 = 32 * (h % 4)
            hg = h // 4
            amask_t = None
            if (br == 0 and qc in (0, 15)) or (br == 1 and qc == 0):
                if qc not in amask_tiles:
                    amask_tiles[qc] = ctx.const_scoped(f"amask{br}q{qc}")
                amask_t = amask_tiles[qc]
            ctx.dma("sp", bias_sb[sb_][:, 0:NW], d["a0_bias"][br, h, qc, :, 0:NW],
                    reads=[], writes=[bias_sb[sb_].r()], group=f"a0b{sb_}")
            for bi, bst in enumerate((s0, s1)[:nband]):
                ctx.mm(ctx.pB[:, bi * 512:(bi + 1) * 512],
                       qb[b0:b0 + 32, hg * L + qc * 128: hg * L + (qc + 1) * 128],
                       kb[b0:b0 + 32, hg * L + bst: hg * L + bst + 512],
                       qbr + kbr, ("pB", bi), start=True, stop=False,
                       tag="a0sc", tile_position=(b0, 0))
                last = amask_t is None
                ctx.mm(ctx.pB[:, bi * 512:(bi + 1) * 512], ctx.const["ident"][:],
                       bias_sb[sb_][:, bi * 512:(bi + 1) * 512],
                       [("const", "ident"), bias_sb[sb_].r()], ("pB", bi),
                       start=False, stop=last, tag="a0bi")
                if amask_t is not None:
                    ctx.mm(ctx.pB[:, bi * 512:(bi + 1) * 512], ctx.const["ident"][:],
                           amask_t[:, bi * 512:(bi + 1) * 512],
                           [("const", "ident"), amask_t.r()], ("pB", bi),
                           start=False, stop=True, tag="a0am")
            pBb = [("pB", bi) for bi in range(nband)]
            ctx.red(mx[rp][:], ctx.pB[:, 0:NW], ALU.max, pBb, [mx[rp].r()], negate=True)
            ctx.actv(po[rp][:, 0:NW], ctx.pB[:, 0:NW], AF.Exp, pBb + [mx[rp].r()],
                     [po[rp].r(), den[rp].r()], bias=mx[rp][:], accum=den[rp][:])
            ctx.recip(den[rp][:], den[rp][:], [den[rp].r()], [den[rp].r()])
            ctx.ts(po[rp][:, 0:NW], po[rp][:, 0:NW], den[rp][:], None, ALU.mult, None,
                   [po[rp].r(), den[rp].r()], [po[rp].r()])

        def back_tr(g):
            qc, h, s0, s1, nband = meta(g)
            rp = g % NR0
            rt = g % NRT0
            for batch in range(nband):
                bank = (g + batch) % 3
                for jc4 in range(4):
                    jc = batch * 4 + jc4
                    psl = slice(bank * 512 + jc4 * 128, bank * 512 + (jc4 + 1) * 128)
                    ctx.mm(ctx.pA[:, psl], po[rp][:, jc * 128:(jc + 1) * 128],
                           ctx.const["ident"][:], [po[rp].r(), ("const", "ident")],
                           ("pA", bank), tag="a0pT")
                ctx.cp("act" if (g + batch) % 2 == 0 else "ve",
                       poT[rt][:, batch * 512:(batch + 1) * 512],
                       ctx.pA[:, bank * 512:(bank + 1) * 512],
                       [("pA", bank)], [poT[rt].r(batch)])

        def back_av(g):
            qc, h, s0, s1, nband = meta(g)
            rt = g % NRT0
            b0 = 32 * (h % 4)
            hg = h // 4
            pairs = []
            for jc in range(4 * nband):
                bst = s0 if jc < 4 else s1
                kchunk = bst // 128 + (jc % 4)
                pairs.append((vt[:, kchunk * 256 + 32 * h: kchunk * 256 + 32 * h + 32],
                              poT[rt][:, jc * 128:(jc + 1) * 128]))
            ctx.mm_acc(ctx.pA[b0:b0 + 32, av_sl], pairs,
                       reads=vtr + [poT[rt].r(b) for b in range(nband)],
                       wregion=("pA", 3), tag="a0av", tile_position=(0, b0))
            if h % 4 == 3:
                ctx.cp("act", bout[:, hg * T + qc * 128: hg * T + (qc + 1) * 128],
                       ctx.pA[:, av_sl], [("pA", 3)], [bout.r(hg)])

        NGr = len(groups)
        for gi in range(NGr + 3):
            if gi >= 3:
                back_av(gi - 3)
            if 2 <= gi < NGr + 2:
                back_tr(gi - 2)
            if gi < NGr:
                front(gi)
        if br == 0:
            ctx.dbg("qb", qb[:], qbr)
            ctx.dbg("kb", kb[:], kbr)
            ctx.dbg("vt", vt[:], vtr)
            ctx.dbg("bout", bout[:], [bout.r(0), bout.r(1)])
        projb = load_pc(ctx, "a0_projb_pc", 4)
        wp = ctx.sb([128, 256], BF16, "wp")
        for oc in range(2):
            ctx.dma("pool", wp[:].rearrange("p (k m) -> p k m", m=128),
                    d["a0_proj_w"][br][:, oc * 128:(oc + 1) * 128]
                    .rearrange("(k p) m -> p k m", p=128),
                    reads=[], writes=[wp.r()], group="a0w")
            for nb in range(4):
                nsl = slice(nb * 512, (nb + 1) * 512)
                ctx.mm_acc(ctx.pA[:, nsl],
                           [(wp[:, kc * 128:(kc + 1) * 128],
                             bout[:, kc * T + nb * 512: kc * T + (nb + 1) * 512]) for kc in range(2)],
                           reads=[wp.r(), bout.r(0), bout.r(1)], wregion=("pA", nb), tag="a0pr")
                oci = 2 * br + oc
                ctx.ts(pout[:, oci * T + nb * 512: oci * T + (nb + 1) * 512], ctx.pA[:, nsl],
                       projb[:, br * 2 + oc: br * 2 + oc + 1], None, ALU.add, None,
                       [("pA", nb), projb.r()], [poutr[oci]])


def stage_l0attn(ctx):
    d = ctx.dram
    X = ctx.X
    Xr = [X.r(c) for c in range(8)]
    with ctx.scoped():
        f = ln_factors(ctx, X, Xr)
        A, B = f["A"], f["B"]
        pout = ctx.sb([128, 4 * T], BF16, "pout")
        poutr = [pout.r(c) for c in range(4)]
        with ctx.scoped():
            xin = ctx.sb([128, 4 * T], BF16, "xin")
            xinr = [xin.r(c) for c in range(4)]
            with ctx.scoped():
                downb = load_pc(ctx, "a0_downb_pc", 4)
                n1w = load_pc(ctx, "a0_n1w_pc", 4)
                wd = ctx.sb([128, 8 * 512], BF16, "wdn")
                t1 = ctx.sb([128, 1024], F32, "dt1")
                xa = ctx.sb([128, 1024], BF16, "dxa")
                sqb = ctx.sb([128, T], BF16, "dsq")
                fac = ctx.sb([128, T], F32, "dfac")
                ctx.dma("pool", wd[:].rearrange("p (k m) -> p k m", m=512),
                        d["a0_down_w"].rearrange("(k p) m -> p k m", p=128),
                        reads=[], writes=[wd.r()], group="a0w")
                pr = {0: ("pA", 0), 1: ("pA", 2), 2: ("pB", 0), 3: ("pB", 2)}
                prr = {0: [("pA", 0), ("pA", 1)], 1: [("pA", 2), ("pA", 3)],
                       2: [("pB", 0), ("pB", 1)], 3: [("pB", 2), ("pB", 3)]}
                pap = {0: ctx.pA, 1: ctx.pA, 2: ctx.pB, 3: ctx.pB}
                pof = {0: 0, 1: 1024, 2: 0, 3: 1024}
                for th in range(2):
                    t0 = th * 1024
                    for kc in range(8):
                        ctx.tt(t1[:], X[:, kc * T + t0: kc * T + t0 + 1024], A[:, t0:t0 + 1024],
                               ALU.subtract, [Xr[kc], A.r()], [t1.r()])
                        ctx.tt(xa[:], t1[:], B[:, t0:t0 + 1024], ALU.mult,
                               [t1.r(), B.r()], [xa.r()])
                        for oc in range(4):
                            for n2 in range(2):
                                po = pof[oc] + n2 * 512
                                ctx.mm(pap[oc][:, po:po + 512],
                                       wd[:, kc * 512 + oc * 128: kc * 512 + (oc + 1) * 128],
                                       xa[:, n2 * 512:(n2 + 1) * 512],
                                       [wd.r(), xa.r()], prr[oc][n2],
                                       start=(kc == 0), stop=(kc == 7), tag="a0dn")
                    for oc in range(4):
                        ctx.ts(xin[:, oc * T + t0: oc * T + t0 + 1024],
                               pap[oc][:, pof[oc]:pof[oc] + 1024], downb[:, oc:oc + 1], None,
                               ALU.add, None, prr[oc] + [downb.r()], [xinr[oc]])
                for oc in range(4):
                    ctx.actv(sqb[:], xin[:, oc * T:(oc + 1) * T], AF.Square,
                             [xinr[oc]], [sqb.r()])
                    for nb in range(4):
                        ctx.mm(ctx.pB[:, nb * 512:(nb + 1) * 512], ctx.const["ones128"][:],
                               sqb[:, nb * 512:(nb + 1) * 512],
                               [("const", "ones128"), sqb.r()], ("pB", nb),
                               start=(oc == 0), stop=(oc == 3), tag="a0S2")
                pBr = [("pB", nb) for nb in range(4)]
                ctx.ts(fac[:], ctx.pB[:, 0:T], 1.0 / 512, 1e-6, ALU.mult, ALU.add, pBr, [fac.r()])
                ctx.actv(fac[:], fac[:], AF.Ln, [fac.r()], [fac.r()])
                ctx.actv(fac[:], fac[:], AF.Exp, [fac.r()], [fac.r()], scale=-0.5)
                for oc in range(4):
                    sl = slice(oc * T, (oc + 1) * T)
                    ctx.stt(xin[:, sl], xin[:, sl], n1w[:, oc:oc + 1], fac[:], ALU.mult, ALU.mult,
                            [xinr[oc], n1w.r(), fac.r()], [xinr[oc]])
                ctx.dbg("xin", xin[:], xinr)
            ctx.P.barrier()
            _l0_branch(ctx, 0, xin, xinr, pout, poutr)
            ctx.P.barrier()
            _l0_branch(ctx, 1, xin, xinr, pout, poutr)
            ctx.P.barrier()
        ctx.dbg("pout", pout[:], poutr)
        ctx.P.barrier()
        xint = ctx.sb([128, 8 * T], BF16, "xint")
        xir = [xint.r(c) for c in range(8)]
        with ctx.scoped():
            upb = load_pc(ctx, "a0_upb_pc", 8)
            wu = ctx.sb([128, 512], BF16, "wup")
            t1 = ctx.sb([128, T], F32, "ut1")
            t2 = ctx.sb([128, T], F32, "ut2")
            tu = ctx.sb([128, 512], F32, "utu")
            for oc in range(8):
                ctx.dma("pool", wu[:].rearrange("p (k m) -> p k m", m=128),
                        d["a0_up_w"][:, oc * 128:(oc + 1) * 128].rearrange("(k p) m -> p k m", p=128),
                        reads=[], writes=[wu.r()], group="a0w")
                sl = slice(oc * T, (oc + 1) * T)
                ctx.tt(t1[:], X[:, sl], A[:], ALU.subtract, [Xr[oc], A.r()], [t1.r()])
                ctx.tt(t2[:], t1[:], B[:], ALU.mult, [t1.r(), B.r()], [t2.r()])
                for nb in range(4):
                    nsl = slice(nb * 512, (nb + 1) * 512)
                    ctx.mm_acc(ctx.pA[:, nsl],
                               [(wu[:, kc * 128:(kc + 1) * 128],
                                 pout[:, kc * T + nb * 512: kc * T + (nb + 1) * 512]) for kc in range(4)],
                               reads=[wu.r()] + poutr, wregion=("pA", nb), tag="a0up")
                    ctx.stt(xint[:, oc * T + nb * 512: oc * T + (nb + 1) * 512],
                            ctx.pA[:, nsl], upb[:, oc:oc + 1],
                            t2[:, nsl], ALU.add, ALU.add,
                            [("pA", nb), upb.r(), t2.r()], [xir[oc]])
                    ctx.stt(tu[:], ctx.pA[:, nsl], upb[:, oc:oc + 1], t2[:, nsl],
                            ALU.add, ALU.add, [("pA", nb), upb.r(), t2.r()], [tu.r()])
                    xsl = slice(oc * T + nb * 512, oc * T + (nb + 1) * 512)
                    ctx.tt(X[:, xsl], X[:, xsl], tu[:], ALU.add, [Xr[oc], tu.r()], [Xr[oc]])
        ctx.P.barrier()
        with ctx.scoped():
            n2w = load_pc(ctx, "a0_n2w_pc", 8)
            b1 = load_pc(ctx, "a0_b1_pc", 16)
            b2 = load_pc(ctx, "a0_b2_pc", 8)
            h2 = ctx.sb([128, 8 * T], BF16, "h2")
            h2r = [h2.r(c) for c in range(8)]
            with ctx.scoped():
                sqb = ctx.sb([128, T], BF16, "fsq0")
                fac = ctx.sb([128, T], F32, "fac0")
                for c in range(8):
                    sl = slice(c * T, (c + 1) * T)
                    ctx.actv(sqb[:], xint[:, sl], AF.Square, [xir[c]], [sqb.r()])
                    for nb in range(4):
                        osl = slice(nb * 512, (nb + 1) * 512)
                        ctx.mm(ctx.pB[:, osl], ctx.const["ones128"][:], sqb[:, osl],
                               [("const", "ones128"), sqb.r()], ("pB", nb),
                               start=(c == 0), stop=(c == 7), tag="a0S2b")
                pBr = [("pB", nb) for nb in range(4)]
                ctx.ts(fac[:], ctx.pB[:, 0:T], 1.0 / 1024, 1e-6, ALU.mult, ALU.add, pBr, [fac.r()])
                ctx.actv(fac[:], fac[:], AF.Ln, [fac.r()], [fac.r()])
                ctx.actv(fac[:], fac[:], AF.Exp, [fac.r()], [fac.r()], scale=-0.5)
                for c in range(8):
                    sl = slice(c * T, (c + 1) * T)
                    ctx.stt(h2[:, sl], xint[:, sl], n2w[:, c:c + 1], fac[:], ALU.mult, ALU.mult,
                            [xir[c], n2w.r(), fac.r()], [h2r[c]])
            ctx.P.barrier()
            gb = ctx.sb([128, 16 * 512], BF16, "gb")
            gbr = [gb.r(i) for i in range(16)]
            w1s = [ctx.sb([128, 1024], BF16, "w1s0") for _ in range(2)]
            w2s = [ctx.sb([128, 16 * 128], BF16, "w2s0") for _ in range(2)]
            for th in range(4):
                t0 = th * 512
                for i in range(16):
                    s = i % 2
                    ctx.dma("pool", w1s[s][:].rearrange("p (k m) -> p k m", m=128),
                            d["a0_ffn_w1"][:, i * 128:(i + 1) * 128].rearrange("(k p) m -> p k m", p=128),
                            reads=[], writes=[w1s[s].r()], group=f"a0w1{s}")
                    ctx.mm_acc(ctx.pA[:, 0:512],
                               [(w1s[s][:, kc * 128:(kc + 1) * 128],
                                 h2[:, kc * T + t0: kc * T + t0 + 512])
                                for kc in range(8)],
                               reads=[w1s[s].r()] + h2r, wregion=("pA", 0), tag="a0f1")
                    ctx.actv(gb[:, i * 512:(i + 1) * 512], ctx.pA[:, 0:512], AF.Gelu,
                             [("pA", 0), b1.r()], [gbr[i]],
                             bias=b1[:, i:i + 1])
                for oc in range(8):
                    s = oc % 2
                    ctx.dma("pool", w2s[s][:].rearrange("p (k m) -> p k m", m=128),
                            d["a0_ffn_w2"][:, oc * 128:(oc + 1) * 128].rearrange("(k p) m -> p k m", p=128),
                            reads=[], writes=[w2s[s].r()], group=f"a0w2{s}")
                    ctx.mm_acc(ctx.pB[:, 0:512],
                               [(w2s[s][:, kc * 128:(kc + 1) * 128],
                                 gb[:, kc * 512: (kc + 1) * 512])
                                for kc in range(16)],
                               reads=[w2s[s].r()] + gbr, wregion=("pB", 0), tag="a0f2")
                    xsl = slice(oc * T + t0, oc * T + t0 + 512)
                    ctx.stt(X[:, xsl], ctx.pB[:, 0:512], b2[:, oc:oc + 1],
                            X[:, xsl], ALU.add, ALU.add,
                            [("pB", 0), b2.r(), Xr[oc]], [Xr[oc]])


stage_l0attn.needs = ["a0_down_w", "a0_downb_pc", "a0_n1w_pc", "a0_n2w_pc",
                      "a0_qkv_w", "a0_qkvb_pc", "a0_bv", "a0_proj_w", "a0_projb_pc",
                      "a0_up_w", "a0_upb_pc", "a0_ffn_w1", "a0_b1_pc",
                      "a0_ffn_w2", "a0_b2_pc", "a0_bias"]



def _shifted_copy(ctx, dst, dst_regions, y, yr, dtype_note=""):
    for c in range(8):
        sl = slice(c * T, (c + 1) * T)
        eng = "ve" if c % 2 == 0 else "act"
        if c < 4:
            ctx.cp(eng, dst[:, sl], y[:, sl], [yr[c]], [dst_regions[c]])
        else:
            ctx.memset(dst[:, c * T: c * T + 1], 0.0, [dst_regions[c]], eng="pool")
            ctx.cp(eng, dst[:, c * T + 1:(c + 1) * T], y[:, c * T:(c + 1) * T - 1],
                   [yr[c]], [dst_regions[c]])


def stage_l1ffn(ctx):
    d = ctx.dram
    X = ctx.X
    Xr = [X.r(c) for c in range(8)]
    XrAll = Xr
    Xbf = X[:].bitcast(BF16)
    nc = ctx.nc
    with ctx.scoped():
        spname = ctx.nm("xspill")
        scratch_r = [X.r("s0"), X.r("s1")]
        sp = nc.dram_tensor(spname, [128, 8 * T], F32, kind="Internal").ap()
        ctx.dma("sp", sp[:, 0:4 * T], X[:, 0:4 * T], reads=Xr[0:4] + scratch_r,
                writes=[("dram", spname, 0)], group="xspill")
        ctx.dma("sp", sp[:, 4 * T:8 * T], X[:, 4 * T:8 * T], reads=Xr[4:8] + scratch_r,
                writes=[("dram", spname, 1)], group="xspill")
        y = ctx.sb([128, 8 * T], F32, "y")
        yr = [y.r(c) for c in range(8)]
        with ctx.scoped():
            f = ln_factors(ctx, X, Xr)
            A, B = f["A"], f["B"]
            t1 = ctx.sb([128, T], F32, "t1")
            t2 = ctx.sb([128, T], F32, "t2")
            for c in range(8):
                sl = slice(c * T, (c + 1) * T)
                eng = "ve"
                tl = t1 if c % 2 == 0 else t2
                ctx.tt(tl[:], X[:, sl], A[:], ALU.subtract, [Xr[c], A.r()], [tl.r()], eng=eng)
                ctx.tt(y[:, sl], tl[:], B[:], ALU.mult, [tl.r(), B.r()], [yr[c]], eng=eng)
        ctx.P.barrier()

        for li in range(6):
            _l1ffn_attn_sublayer(ctx, li, y, yr, Xbf, XrAll)
            ctx.P.barrier()
            _l1ffn_ffn_sublayer(ctx, li, y, yr, Xbf, XrAll)
            ctx.P.barrier()

        ctx.dma("sp", X[:, 0:4 * T], sp[:, 0:4 * T], reads=[("dram", spname, 0)],
                writes=Xr[0:4] + scratch_r, group="xrest")
        ctx.dma("sp", X[:, 4 * T:8 * T], sp[:, 4 * T:8 * T], reads=[("dram", spname, 1)],
                writes=Xr[4:8] + scratch_r, group="xrest")
        for c in range(8):
            sl = slice(c * T, (c + 1) * T)
            ctx.stt(X[:, sl], y[:, sl], 0.5, X[:, sl], ALU.mult, ALU.add,
                    [yr[c], Xr[c]], [Xr[c]])


def _l1ffn_attn_sublayer(ctx, li, y, yr, Xbf, XrAll):
    d = ctx.dram
    pAr = [("pA", nb) for nb in range(4)]
    pBr = [("pB", nb) for nb in range(4)]
    with ctx.scoped():
        tabs = {}
        for tn in ("rt_qcs", "rt_qss", "rt_kci", "rt_ksi"):
            tt_ = ctx.sb([128, T], BF16, tn)
            ctx.dma("sp", tt_[:], d[tn], reads=[], writes=[tt_.r()], group="small")
            tabs[tn] = tt_
        qs_col = ctx.sb([128, 1], F32, "qs")
        ks_col = ctx.sb([128, 1], F32, "ks")
        ctx.dma("sp", qs_col[:], d["f1_qs"][li].unsqueeze(1), reads=[], writes=[qs_col.r()], group="small")
        ctx.dma("sp", ks_col[:], d["f1_ks"][li].unsqueeze(1), reads=[], writes=[ks_col.r()], group="small")
        class _V:
            def __init__(self, ap):
                self.ap = ap
            def __getitem__(self, key):
                return self.ap[key]
        xs = _V(Xbf[:, 8 * T:16 * T])
        xsr = [ctx.X.r("s1")] * 8
        _shifted_copy(ctx, xs, xsr, y, yr)
        xsr = [ctx.X.r("s1")]
        aout = Xbf[:, 0:8 * T]
        aout_r = [ctx.X.r("s0")]

        qft = ctx.sb([128, T], BF16, "qft")
        kp = ctx.sb([128, 2304], BF16, "kp")
        vp = ctx.sb([128, 18 * 128], BF16, "vp")
        ctx.memset(kp[:, 0:256], 0.0, [kp.r("z")], eng="pool")
        ctx.memset(vp[:, 0:256], 0.0, [vp.r("z")], eng="pool")
        sq = ctx.sb([128, T], BF16, "sq")
        nrmq = ctx.sb([128, T], BF16, "nrmq")
        nrmk = ctx.sb([128, T], BF16, "nrmk")
        qhb = ctx.sb([128, T], BF16, "qhb")
        qraw = ctx.sb([128, T], BF16, "qraw")
        kraw = ctx.sb([128, T], BF16, "kraw")
        tmp1 = sq
        eps24 = ctx.sb([128, 1], F32, "eps24")
        ctx.memset(eps24[:], 1e-24, [eps24.r()], eng="pool")
        NRING = 5
        NRT = 3
        ptl = [ctx.sb([128, 384], BF16, "ptl") for _ in range(NRING)]
        ptlT = [ctx.sb([128, 384], BF16, "ptlT") for _ in range(NRT)]
        mx = [ctx.sb([128, 1], F32, "mx") for _ in range(NRING)]
        den = [ctx.sb([128, 1], F32, "den") for _ in range(NRING)]
        wsl = [ctx.sb([128, 3 * 1024], BF16, "wq1") for _ in range(2)]

        def l2norm_phase(raw, raw_r, nrm, psum, pname):
            pr4 = [(pname, nb) for nb in range(4)]
            ctx.actv(sq[:], raw[:], AF.Square, [raw_r], [sq.r()])
            for nb in range(4):
                ctx.mm(psum[:, nb * 512:(nb + 1) * 512], ctx.const["block64"][:],
                       sq[:, nb * 512:(nb + 1) * 512], [("const", "block64"), sq.r()],
                       (pname, nb), tag="nrm")
            ctx.actv(nrm[:], psum[:, 0:T], AF.Ln, pr4 + [eps24.r()], [nrm.r()],
                     bias=eps24[:])
            ctx.actv(nrm[:], nrm[:], AF.Exp, [nrm.r()], [nrm.r()], scale=-0.5)

        def l2rot_phase(raw, raw_r, nrm, gain_col, gain_r, cs, ss, outt, wout):
            pBr4 = [("pB", nb) for nb in range(4)]
            ctx.stt(qhb[:], raw[:], gain_col[:], nrm[:], ALU.mult, ALU.mult,
                    [raw_r, gain_r, nrm.r()], [qhb.r()])
            for nb in range(4):
                ctx.mm(ctx.pB[:, nb * 512:(nb + 1) * 512], ctx.const["rotperm"][:],
                       qhb[:, nb * 512:(nb + 1) * 512], [("const", "rotperm"), qhb.r()],
                       ("pB", nb), tag="rot")
            ctx.cp("act", raw[:], ctx.pB[:, 0:T], pBr4, [raw_r])
            ctx.tt(tmp1[:], qhb[:], cs[:], ALU.mult, [qhb.r(), cs.r()], [tmp1.r()])
            ctx.tt(nrm[:], raw[:], ss[:], ALU.mult, [raw_r, ss.r()], [nrm.r()])
            ctx.tt(outt, tmp1[:], nrm[:], ALU.add, [tmp1.r(), nrm.r()], wout)

        for hc in range(8):
            s = hc % 2
            for part in range(3):
                ctx.dma("pool", wsl[s][:, part * 1024:(part + 1) * 1024].rearrange("p (k m) -> p k m", m=128),
                        d["f1_wqkv"][li, part * 8 + hc],
                        reads=[], writes=[wsl[s].r()], group=f"q1{s}")
            wr = [wsl[s].r()]
            for nb in range(4):
                nsl = slice(nb * 512, (nb + 1) * 512)
                ctx.mm_acc(ctx.pA[:, nsl],
                           [(wsl[s][:, kc * 128:(kc + 1) * 128],
                             xs[:, kc * T + nb * 512: kc * T + (nb + 1) * 512]) for kc in range(8)],
                           reads=wr + xsr, wregion=("pA", nb), tag="q1")
            ctx.cp("ve", qraw[:], ctx.pA[:, 0:T], pAr, [qraw.r()])
            for nb in range(4):
                nsl = slice(nb * 512, (nb + 1) * 512)
                ctx.mm_acc(ctx.pA[:, nsl],
                           [(wsl[s][:, 1024 + kc * 128: 1024 + (kc + 1) * 128],
                             xs[:, kc * T + nb * 512: kc * T + (nb + 1) * 512]) for kc in range(8)],
                           reads=wr + xsr, wregion=("pA", nb), tag="k1")
            l2norm_phase(qraw, qraw.r(), nrmq, ctx.pB, "pB")
            ctx.cp("ve", kraw[:], ctx.pA[:, 0:T], pAr, [kraw.r()])
            ctx.actv(sq[:], kraw[:], AF.Square, [kraw.r()], [sq.r()])
            for tc in range(16):
                bank = tc % 4
                psl = slice(bank * 512, bank * 512 + 128)
                ctx.mm_acc(ctx.pA[:, psl],
                           [(xs[:, kc * T + tc * 128: kc * T + (tc + 1) * 128],
                             wsl[s][:, 2048 + kc * 128: 2048 + (kc + 1) * 128]) for kc in range(8)],
                           reads=wr + xsr, wregion=("pA", bank), tag="v1")
                ctx.cp("act" if tc % 2 else "ve", vp[:, (tc + 2) * 128:(tc + 3) * 128],
                       ctx.pA[:, psl], [("pA", bank)], [vp.r(tc)])
            pr4 = [("pA", nb) for nb in range(4)]
            for nb in range(4):
                ctx.mm(ctx.pA[:, nb * 512:(nb + 1) * 512], ctx.const["block64"][:],
                       sq[:, nb * 512:(nb + 1) * 512], [("const", "block64"), sq.r()],
                       ("pA", nb), tag="nrm")
            ctx.actv(nrmk[:], ctx.pA[:, 0:T], AF.Ln, pr4 + [eps24.r()], [nrmk.r()],
                     bias=eps24[:])
            ctx.actv(nrmk[:], nrmk[:], AF.Exp, [nrmk.r()], [nrmk.r()], scale=-0.5)
            l2rot_phase(qraw, qraw.r(), nrmq, qs_col, qs_col.r(),
                        tabs["rt_qcs"], tabs["rt_qss"], qft[:], [qft.r()])
            l2rot_phase(kraw, kraw.r(), nrmk, ks_col, ks_col.r(),
                        tabs["rt_kci"], tabs["rt_ksi"], kp[:, 256:256 + T], [kp.r("k")])
            vpr = [vp.r(tc) for tc in range(16)] + [vp.r("z")]
            groups = [(wi, qc2, j) for wi in range(8) for qc2 in range(2) for j in range(2)]
            LOOK = 4

            def front(g):
                wi, qc2, j = groups[g]
                b0 = 64 * j
                q0 = wi * 256 + qc2 * 128
                bank = g % 4
                ssl = slice(bank * 512, bank * 512 + 384)
                rg = g % NRING
                ctx.mm(ctx.pB[:, ssl], qft[b0:b0 + 64, q0:q0 + 128],
                       kp[b0:b0 + 64, q0: q0 + 384],
                       [qft.r(), kp.r("k"), kp.r("z")], ("pB", bank),
                       start=True, stop=False, tag="sc1")
                mvar = f"lmask{1 if wi == 0 else 0}{qc2}"
                ctx.mm(ctx.pB[:, ssl], ctx.const["ident"][:], ctx.const[mvar][:],
                       [("const", "ident"), ("const", mvar)], ("pB", bank),
                       start=False, stop=True, tag="msk")
                ctx.red(mx[rg][:], ctx.pB[:, ssl], ALU.max, [("pB", bank)], [mx[rg].r()],
                        negate=True)
                ctx.actv(ptl[rg][:], ctx.pB[:, ssl], AF.Exp, [("pB", bank), mx[rg].r()],
                         [ptl[rg].r(), den[rg].r()], bias=mx[rg][:], accum=den[rg][:])
                ctx.recip(den[rg][:], den[rg][:], [den[rg].r()], [den[rg].r()])
                ctx.ts(ptl[rg][:], ptl[rg][:], den[rg][:], None, ALU.mult, None,
                       [ptl[rg].r(), den[rg].r()], [ptl[rg].r()])

            def back_tr(g):
                rg = g % NRING
                rt = g % NRT
                tb = g % 2
                for jc in range(3):
                    psl = slice(tb * 512 + jc * 128, tb * 512 + (jc + 1) * 128)
                    ctx.mm(ctx.pA[:, psl], ptl[rg][:, jc * 128:(jc + 1) * 128],
                           ctx.const["ident"][:], [ptl[rg].r(), ("const", "ident")],
                           ("pA", tb), tag="pT1")
                ctx.cp("act" if g % 2 == 0 else "ve", ptlT[rt][:],
                       ctx.pA[:, tb * 512: tb * 512 + 384],
                       [("pA", tb)], [ptlT[rt].r(jc) for jc in range(3)])

            def back_av(g):
                wi, qc2, j = groups[g]
                b0 = 64 * j
                rt = g % NRT
                avbank = 2 + ((g // 2) % 2)
                av_sl2 = slice(avbank * 512, avbank * 512 + 128)
                ctx.mm_acc(ctx.pA[b0:b0 + 64, av_sl2],
                           [(vp[:, (2 * wi + qc2 + jc) * 128 + b0:
                                 (2 * wi + qc2 + jc) * 128 + b0 + 64],
                             ptlT[rt][:, jc * 128:(jc + 1) * 128]) for jc in range(3)],
                           reads=vpr + [ptlT[rt].r(jc) for jc in range(3)],
                           wregion=("pA", avbank), tag="av1", tile_position=(0, b0))
                if j == 1:
                    ctx.cp("act", aout[:, hc * T + wi * 256 + qc2 * 128:
                                       hc * T + wi * 256 + (qc2 + 1) * 128],
                           ctx.pA[:, av_sl2], [("pA", avbank)], aout_r)

            NG = len(groups)
            for gi in range(NG + LOOK + 2):
                if gi >= LOOK + 2:
                    back_av(gi - LOOK - 2)
                if LOOK <= gi < NG + LOOK:
                    back_tr(gi - LOOK)
                if gi < NG:
                    front(gi)
        for oc in range(8):
            s = oc % 2
            ps = ctx.pA if oc % 2 == 0 else ctx.pB
            pn = "pA" if oc % 2 == 0 else "pB"
            ctx.dma("pool", wsl[s][:, 0:1024].rearrange("p (k m) -> p k m", m=128),
                    d["f1_wo"][li, oc],
                    reads=[], writes=[wsl[s].r()], group=f"q1{s}")
            for nb in range(4):
                nsl = slice(nb * 512, (nb + 1) * 512)
                ctx.mm_acc(ps[:, nsl],
                           [(wsl[s][:, kc * 128:(kc + 1) * 128],
                             aout[:, kc * T + nb * 512: kc * T + (nb + 1) * 512]) for kc in range(8)],
                           reads=[wsl[s].r()] + aout_r, wregion=(pn, nb), tag="wo1")
                ysl = slice(oc * T + nb * 512, oc * T + (nb + 1) * 512)
                ctx.tt(y[:, ysl], y[:, ysl], ps[:, nsl], ALU.add,
                       [yr[oc], (pn, nb)], [yr[oc]])


def _l1ffn_ffn_sublayer(ctx, li, y, yr, Xbf, XrAll):
    d = ctx.dram
    DI2 = 2816
    with ctx.scoped():
        g_pc = load_pc(ctx, "f1_lng", 8 * 6)
        b_pc = load_pc(ctx, "f1_lnb", 8 * 6)
        b1_pc = load_pc(ctx, "f1_b1", 44 * 6)
        b2_pc = load_pc(ctx, "f1_b2", 8 * 6)
        xn2 = ctx.sb([128, 8 * T], BF16, "xn2")
        xnr = [xn2.r(c) for c in range(8)]
        onesr = ("const", "ones128")
        ones = ctx.const["ones128"]
        with ctx.scoped():
            sqb = ctx.sb([128, T], BF16, "fsq")
            A = ctx.sb([128, T], F32, "fA")
            B = ctx.sb([128, T], F32, "fB")
            tmp = ctx.sb([128, T], F32, "ftmp")
            _shifted_copy(ctx, xn2, xnr, y, yr)
            for c in range(8):
                sl = slice(c * T, (c + 1) * T)
                ctx.actv(sqb[:], xn2[:, sl], AF.Square, [xnr[c]], [sqb.r()])
                for nb in range(4):
                    osl = slice(nb * 512, (nb + 1) * 512)
                    ctx.mm(ctx.pA[:, osl], ones[:], xn2[:, c * T + nb * 512: c * T + (nb + 1) * 512],
                           [onesr, xnr[c]], ("pA", nb), start=(c == 0), stop=(c == 7), tag="fS1")
                    ctx.mm(ctx.pB[:, osl], ones[:], sqb[:, osl], [onesr, sqb.r()],
                           ("pB", nb), start=(c == 0), stop=(c == 7), tag="fS2")
            pAr = [("pA", nb) for nb in range(4)]
            pBr = [("pB", nb) for nb in range(4)]
            ctx.ts(A[:], ctx.pA[:, 0:T], 1.0 / 1024, None, ALU.mult, None, pAr, [A.r()])
            ctx.tt(tmp[:], A[:], A[:], ALU.mult, [A.r()], [tmp.r()])
            ctx.ts(B[:], ctx.pB[:, 0:T], 1.0 / 1024, 1e-5, ALU.mult, ALU.add, pBr, [B.r()])
            ctx.tt(tmp[:], B[:], tmp[:], ALU.subtract, [B.r(), tmp.r()], [tmp.r()])
            ctx.actv(B[:], tmp[:], AF.Ln, [tmp.r()], [B.r()])
            ctx.actv(B[:], B[:], AF.Exp, [B.r()], [B.r()], scale=-0.5)
            tmp2 = ctx.sb([128, T], F32, "ftmp2")
            for c in range(8):
                sl = slice(c * T, (c + 1) * T)
                eng = "ve"
                tl = tmp if c % 2 == 0 else tmp2
                ctx.tt(tl[:], xn2[:, sl], A[:], ALU.subtract, [xnr[c], A.r()], [tl.r()],
                       eng=eng)
                ctx.tt(tl[:], tl[:], B[:], ALU.mult, [tl.r(), B.r()], [tl.r()], eng=eng)
                ctx.ts(xn2[:, sl], tl[:], g_pc[:, li * 8 + c: li * 8 + c + 1],
                       b_pc[:, li * 8 + c: li * 8 + c + 1],
                       ALU.mult, ALU.add, [tl.r(), g_pc.r(), b_pc.r()], [xnr[c]], eng="ve")
        ctx.P.barrier()
        hbuf = Xbf
        gelb2 = [ctx.sb([128, 1024], BF16, "gelb") for _ in range(2)]
        w1s = [ctx.sb([128, 2 * 1024], BF16, "w1s") for _ in range(4)]
        w2s = [ctx.sb([128, 22 * 128], BF16, "w2s") for _ in range(3)]
        for th in range(2):
            t0 = th * 1024
            for i in range(22):
                s = i % 4
                ctx.dma("pool", w1s[s][:, 0:1024].rearrange("p (k m) -> p k m", m=128),
                        d["f1_w1"][li, i],
                        reads=[], writes=[w1s[s].r()], group=f"w1{s}")
                ctx.dma("pool", w1s[s][:, 1024:2048].rearrange("p (k m) -> p k m", m=128),
                        d["f1_w1"][li, 22 + i],
                        reads=[], writes=[w1s[s].r()], group=f"w1{s}")
                bs = 2 * (i % 2)
                for nb in range(2):
                    nsl = slice((bs + nb) * 512, (bs + nb + 1) * 512)
                    ctx.mm_acc(ctx.pA[:, nsl],
                               [(w1s[s][:, kc * 128:(kc + 1) * 128],
                                 xn2[:, kc * T + t0 + nb * 512: kc * T + t0 + (nb + 1) * 512])
                                for kc in range(8)],
                               reads=[w1s[s].r()] + xnr, wregion=("pA", bs + nb), tag="w1a")
                    ctx.mm_acc(ctx.pB[:, nsl],
                               [(w1s[s][:, 1024 + kc * 128: 1024 + (kc + 1) * 128],
                                 xn2[:, kc * T + t0 + nb * 512: kc * T + t0 + (nb + 1) * 512])
                                for kc in range(8)],
                               reads=[w1s[s].r()] + xnr, wregion=("pB", bs + nb), tag="w1g")
                gelb = gelb2[i % 2]
                ctx.actv(gelb[:], ctx.pB[:, bs * 512: bs * 512 + 1024], AF.Gelu,
                         [("pB", bs), ("pB", bs + 1), b1_pc.r()],
                         [gelb.r()], bias=b1_pc[:, li * 44 + 22 + i: li * 44 + 23 + i])
                ctx.stt(hbuf[:, i * 1024:(i + 1) * 1024], ctx.pA[:, bs * 512: bs * 512 + 1024],
                        b1_pc[:, li * 44 + i: li * 44 + i + 1], gelb[:], ALU.add, ALU.mult,
                        [("pA", bs), ("pA", bs + 1), b1_pc.r(), gelb.r()],
                        [ctx.X.r("s0"), ctx.X.r("s1")])
            for oc in range(8):
                s = oc % 3
                bs = 2 * (oc % 2)
                ctx.dma("pool", w2s[s][:].rearrange("p (k m) -> p k m", m=128),
                        d["f1_w2"][li, oc],
                        reads=[], writes=[w2s[s].r()], group=f"w2{s}")
                for nb in range(2):
                    nsl = slice((bs + nb) * 512, (bs + nb + 1) * 512)
                    ctx.mm_acc(ctx.pA[:, nsl],
                               [(w2s[s][:, kc * 128:(kc + 1) * 128],
                                 hbuf[:, kc * 1024 + nb * 512: kc * 1024 + (nb + 1) * 512])
                                for kc in range(22)],
                               reads=[w2s[s].r(), ctx.X.r("s0"), ctx.X.r("s1")],
                               wregion=("pA", bs + nb), tag="w2")
                    ysl = slice(oc * T + t0 + nb * 512, oc * T + t0 + (nb + 1) * 512)
                    ctx.stt(y[:, ysl], ctx.pA[:, nsl], b2_pc[:, li * 8 + oc: li * 8 + oc + 1],
                            y[:, ysl], ALU.add, ALU.add,
                            [("pA", bs + nb), b2_pc.r(), yr[oc]], [yr[oc]])


stage_l1ffn.needs = ["f1_wqkv", "f1_qs", "f1_ks", "f1_wo", "f1_lng", "f1_lnb",
                     "f1_w1", "f1_b1", "f1_w2", "f1_b2",
                     "rt_qcs", "rt_qss", "rt_kci", "rt_ksi"]



def build_program(stages, final_out=True):
    nc = bass.Bass("TRN2", target_bir_lowering=False)
    dram = {}
    dram["xT"] = nc.dram_tensor("xT", [D, T], F32, kind="ExternalInput").ap()
    need = set()
    for s in stages:
        need.update(s.needs)
    for n in sorted(need):
        shp, dt = INPUT_SPECS[n]
        dram[n] = nc.dram_tensor(n, list(shp), dt, kind="ExternalInput").ap()
    if final_out:
        out_ap = nc.dram_tensor("out", [T, D], F32, kind="ExternalOutput").ap()
    else:
        out_ap = nc.dram_tensor("xout", [D, T], F32, kind="ExternalOutput").ap()

    P = Prog(nc)
    ctx = Ctx(nc, P, dram, host_constants())
    xin = dram["xT"].rearrange("(c p) t -> p c t", p=128)
    x3 = ctx.X[:].rearrange("p (c t) -> p c t", t=T)
    ctx.dma("sp", x3[:, 0:4], xin[:, 0:4], reads=[],
            writes=[ctx.X.r(c) for c in range(4)], group="xload")
    ctx.dma("sp", x3[:, 4:8], xin[:, 4:8], reads=[],
            writes=[ctx.X.r(c) for c in range(4, 8)], group="xload")
    for s in stages:
        s(ctx)
        P.barrier()
    if final_out:
        with ctx.scoped():
            ot = ctx.sb([128, 1024], F32, "ot")
            for tc in range(16):
                for c in range(8):
                    ctx.mm(ctx.pA[:, 0:128] if c % 2 == 0 else ctx.pA[:, 512:640],
                           ctx.X[:, c * T + tc * 128: c * T + (tc + 1) * 128],
                           ctx.const["identf"][:],
                           [ctx.X.r(c), ("const", "identf")], ("pA", c % 2), tag="outT")
                    ctx.cp("ve", ot[:, c * 128:(c + 1) * 128],
                           ctx.pA[:, 0:128] if c % 2 == 0 else ctx.pA[:, 512:640],
                           [("pA", c % 2)], [ot.r(c)])
                ctx.dma("sp", out_ap[tc * 128:(tc + 1) * 128, :], ot[:],
                        reads=[ot.r(c) for c in range(8)], writes=[("dram", "out", tc)],
                        group="xstore")
    else:
        xo = out_ap.rearrange("(c p) t -> p c t", p=128)
        x3 = ctx.X[:].rearrange("p (c t) -> p c t", t=T)
        ctx.dma("sp", xo[:, 0:4], x3[:, 0:4],
                reads=[ctx.X.r(c) for c in range(4)], writes=[("dram", "xo0")], group="xstore")
        ctx.dma("sp", xo[:, 4:8], x3[:, 4:8],
                reads=[ctx.X.r(c) for c in range(4, 8)], writes=[("dram", "xo1")], group="xstore")
    P.op("sp", lambda e: e.nop(),
         reads=[("dram", "out", tc) for tc in range(16)] if final_out
         else [("dram", "xo0"), ("dram", "xo1")],
         writes=[], tag="final_wait")
    P.finalize()
    return nc



INPUT_SPECS = {
    "moe_scale_pc": ((128, 8), F32),
    "moe_routerT": ((D, 8), F32),
    "moe_upT": ((8, 32, 128, 8, 128), F32),
    "moe_downT": ((8, 8, 128, 16, 128), F32),
    "a1_qkvT": ((24, 128, 8, 128), F32),
    "a1_bqkv_pc": ((128, 24), F32),
    "a1_bv_row": ((1, 1024), F32),
    "a1_woT": ((8, 128, 8, 128), F32),
    "a1_bo_pc": ((128, 8), F32),
    "f1_wqkv": ((6, 24, 128, 8, 128), F32),
    "f1_qs": ((6, 128), F32),
    "f1_ks": ((6, 128), F32),
    "f1_wo": ((6, 8, 128, 8, 128), F32),
    "f1_lng": ((128, 48), F32),
    "f1_lnb": ((128, 48), F32),
    "f1_w1": ((6, 44, 128, 8, 128), F32),
    "f1_b1": ((128, 264), F32),
    "f1_w2": ((6, 8, 128, 22, 128), F32),
    "f1_b2": ((128, 48), F32),
    "rt_qcs": ((128, T), BF16),
    "rt_qss": ((128, T), BF16),
    "rt_kci": ((128, T), BF16),
    "rt_ksi": ((128, T), BF16),
    "a0_down_w": ((D, 512), F32),
    "a0_downb_pc": ((128, 4), F32),
    "a0_n1w_pc": ((128, 4), F32),
    "a0_n2w_pc": ((128, 8), F32),
    "a0_qkv_w": ((2, 256, 768), F32),
    "a0_qkvb_pc": ((128, 12), F32),
    "a0_bv": ((2, 1, 256), F32),
    "a0_proj_w": ((2, 256, 256), F32),
    "a0_projb_pc": ((128, 4), F32),
    "a0_up_w": ((512, D), F32),
    "a0_upb_pc": ((128, 8), F32),
    "a0_ffn_w1": ((D, 2048), F32),
    "a0_b1_pc": ((128, 16), F32),
    "a0_ffn_w2": ((2048, D), F32),
    "a0_b2_pc": ((128, 8), F32),
    "a0_bias": ((2, 8, 16, 128, 1024), BF16),
}

L0_GEOM = ((10, 5, 2050), (20, 10, 2060))


def _l0_band_starts(qc):
    s0 = min(max((qc - 1) * 128, 0), L0_L - 512)
    s1 = 1664 if qc != 15 else 0
    return s0, s1


def pc(v):
    v = _f32(v)
    return np.ascontiguousarray(v.reshape(-1, 128).T)


def lin_lhsT(W):
    W = np.asarray(W, np.float32)
    din, dout = W.shape
    a = W.reshape(din // 128, 128, dout // 128, 128)
    return np.ascontiguousarray(a.transpose(2, 1, 0, 3))


def host_prep(inputs):
    s = {}
    s["moe_scale_pc"] = pc(inputs["L0_ffn_norm_scale"])
    s["moe_routerT"] = _f32(np.ascontiguousarray(np.asarray(inputs["L0_ffn_w_router"]).T))
    s["moe_upT"] = np.stack([lin_lhsT(np.asarray(inputs["L0_ffn_w_up"])[e].T)
                             for e in range(8)])
    s["moe_downT"] = np.stack([lin_lhsT(np.asarray(inputs["L0_ffn_w_down"])[e].T)
                               for e in range(8)])
    bqkv = np.asarray(inputs["L1_attn_b_qkv"])
    s["a1_qkvT"] = lin_lhsT(np.asarray(inputs["L1_attn_w_qkv"]).T)
    s["a1_bqkv_pc"] = pc(bqkv)
    s["a1_bv_row"] = _f32(bqkv[2048:3072].reshape(1, 1024))
    s["a1_woT"] = lin_lhsT(np.asarray(inputs["L1_attn_w_o"]).T)
    s["a1_bo_pc"] = pc(inputs["L1_attn_b_o"])

    s["f1_wqkv"] = np.stack([lin_lhsT(np.asarray(inputs["L1_ffn_Wqkv"])[i])
                             for i in range(6)])
    s["f1_qs"] = _f32(np.tile(np.asarray(inputs["L1_ffn_q_scale"]), (1, 2)))
    s["f1_ks"] = _f32(np.tile(np.asarray(inputs["L1_ffn_k_scale"]), (1, 2)))
    s["f1_wo"] = np.stack([lin_lhsT(np.asarray(inputs["L1_ffn_Wo"])[i])
                           for i in range(6)])
    s["f1_lng"] = np.concatenate([pc(np.asarray(inputs["L1_ffn_ln_g"])[i]) for i in range(6)], axis=1)
    s["f1_lnb"] = np.concatenate([pc(np.asarray(inputs["L1_ffn_ln_b"])[i]) for i in range(6)], axis=1)
    W1 = np.asarray(inputs["L1_ffn_W1"], dtype=np.float32)
    W1p = np.zeros((6, D, 5632), np.float32)
    W1p[:, :, 0:2730] = W1[:, :, 0:2730]
    W1p[:, :, 2816:2816 + 2730] = W1[:, :, 2730:5460]
    s["f1_w1"] = np.stack([lin_lhsT(W1p[i]) for i in range(6)])
    b1 = np.asarray(inputs["L1_ffn_b1"], dtype=np.float32)
    b1p = np.zeros((6, 5632), np.float32)
    b1p[:, 0:2730] = b1[:, 0:2730]
    b1p[:, 2816:2816 + 2730] = b1[:, 2730:5460]
    s["f1_b1"] = np.concatenate([pc(b1p[i]) for i in range(6)], axis=1)
    W2 = np.asarray(inputs["L1_ffn_W2"], dtype=np.float32)
    W2p = np.zeros((6, 2816, D), np.float32)
    W2p[:, 0:2730, :] = W2
    s["f1_w2"] = np.stack([lin_lhsT(W2p[i]) for i in range(6)])
    s["f1_b2"] = np.concatenate([pc(np.asarray(inputs["L1_ffn_b2"])[i]) for i in range(6)], axis=1)

    DH, WIN = 64, 256
    half = np.arange(0, DH, 2, dtype=np.float64)
    inv_freq = 1.0 / 10000.0 ** (half / DH)
    t = np.arange(T, dtype=np.float64)
    freqs = t[:, None] * inv_freq[None, :]
    freqs = np.concatenate([freqs, freqs], axis=1)
    base_scale = (half + 0.4 * DH) / (1.4 * DH)
    power = (t - T // 2) / (WIN // 2)
    scale = base_scale[None, :] ** power[:, None]
    scale = np.concatenate([scale, scale], axis=1)
    cs, sn = np.cos(freqs), np.sin(freqs)
    def featmaj(a):
        return _bf(np.tile(a.T.astype(np.float32), (2, 1)))
    s["rt_qcs"] = featmaj(8.0 * cs * scale)
    s["rt_qss"] = featmaj(8.0 * sn * scale)
    s["rt_kci"] = featmaj(cs / scale)
    s["rt_ksi"] = featmaj(sn / scale)

    s["a0_down_w"] = _f32(inputs["L0_attn_down_w"])
    s["a0_downb_pc"] = pc(inputs["L0_attn_down_b"])
    s["a0_n1w_pc"] = pc(inputs["L0_attn_norm1_w"])
    s["a0_n2w_pc"] = pc(inputs["L0_attn_norm2_w"])
    s["a0_qkv_w"] = _f32(np.stack([np.asarray(inputs["L0_attn_qkv1_w"]),
                                   np.asarray(inputs["L0_attn_qkv2_w"])]))
    bq1 = np.asarray(inputs["L0_attn_qkv1_b"], np.float32)
    bq2 = np.asarray(inputs["L0_attn_qkv2_b"], np.float32)
    s["a0_qkvb_pc"] = np.concatenate([pc(bq1[0:512]), pc(bq2[0:512])], axis=1)
    qkvb = np.zeros((128, 12), np.float32)
    for br, bq in enumerate((bq1, bq2)):
        for part in range(2):
            seg = bq[part * 256:(part + 1) * 256]
            qkvb[:, br * 6 + part * 2 + 0] = seg[0:128]
            qkvb[:, br * 6 + part * 2 + 1] = seg[128:256]
    s["a0_qkvb_pc"] = qkvb
    s["a0_bv"] = np.stack([bq1[512:768].reshape(1, 256), bq2[512:768].reshape(1, 256)])
    s["a0_proj_w"] = _f32(np.stack([np.asarray(inputs["L0_attn_proj1_w"]),
                                    np.asarray(inputs["L0_attn_proj2_w"])]))
    pb1 = np.asarray(inputs["L0_attn_proj1_b"], np.float32)
    pb2 = np.asarray(inputs["L0_attn_proj2_b"], np.float32)
    projb = np.zeros((128, 4), np.float32)
    projb[:, 0], projb[:, 1] = pb1[0:128], pb1[128:256]
    projb[:, 2], projb[:, 3] = pb2[0:128], pb2[128:256]
    s["a0_projb_pc"] = projb
    s["a0_up_w"] = _f32(inputs["L0_attn_up_w"])
    s["a0_upb_pc"] = pc(inputs["L0_attn_up_b"])
    s["a0_ffn_w1"] = _f32(inputs["L0_attn_ffn_w1"])
    s["a0_b1_pc"] = pc(inputs["L0_attn_ffn_b1"])
    s["a0_ffn_w2"] = _f32(inputs["L0_attn_ffn_w2"])
    s["a0_b2_pc"] = pc(inputs["L0_attn_ffn_b2"])

    rpbs = (np.asarray(inputs["L0_attn_rpb1"], np.float32),
            np.asarray(inputs["L0_attn_rpb2"], np.float32))
    bias = np.full((2, 8, 16, 128, 1024), -1e38, np.float32)
    for br, (w, shift, Tp) in enumerate(L0_GEOM):
        rpb = rpbs[br]
        for qc in range(16):
            s0, s1 = _l0_band_starts(qc)
            q = qc * 128 + np.arange(128)
            i = (q - shift) % Tp
            for bi, st in enumerate((s0, s1)):
                if bi == 1 and qc not in (0, 15):
                    continue
                k = st + np.arange(512)
                valid = k < Tp
                jj = (k - shift) % Tp
                same = (i[:, None] // w) == (jj[None, :] // w)
                allowed = same & valid[None, :]
                rel = np.clip(i[:, None] - jj[None, :] + w - 1, 0, 2 * w - 2)
                for h in range(8):
                    bias[br, h, qc, :, bi * 512:(bi + 1) * 512] = np.where(
                        allowed, rpb[rel, h], -1e38)
    s["a0_bias"] = _bf(bias)
    return s



def _bf(x):
    return np.ascontiguousarray(x, dtype=BF)

def _f32(x):
    return np.ascontiguousarray(x, dtype=np.float32)


def host_constants():
    c = {}
    c["ident"] = _bf(np.eye(128, dtype=np.float32))
    c["identf"] = _f32(np.eye(128, dtype=np.float32))
    c["onescol_f"] = _f32(np.ones((1, 128), np.float32))
    c["ones128"] = _bf(np.ones((128, 128), np.float32))
    b = np.zeros((128, 128), np.float32)
    b[:64, :64] = 1.0
    b[64:, 64:] = 1.0
    c["block64"] = _bf(b)
    r = np.zeros((128, 128), np.float32)
    for m in range(128):
        if m % 64 < 32:
            r[m + 32, m] = -1.0
        else:
            r[m - 32, m] = 1.0
    c["rotperm"] = _bf(r)
    for edge in range(2):
        for qc2 in range(2):
            m = np.zeros((128, 384), np.float32)
            for p in range(128):
                for n in range(384):
                    dqk = p - n + 256
                    ok = 0 <= dqk <= 256
                    if edge and n < 256 - qc2 * 128:
                        ok = False
                    if not ok:
                        m[p, n] = -1e38
            c[f"lmask{edge}{qc2}"] = _bf(m)
    for br, (w, shift, Tp) in enumerate(L0_GEOM):
        mseg = np.zeros(Tp, np.int32)
        mseg[Tp - w: Tp - shift] = 1
        mseg[Tp - shift:] = 2
        for qc in (0, 15):
            if br == 1 and qc == 15:
                continue
            s0, s1 = _l0_band_starts(qc)
            am = np.zeros((128, 1024), np.float32)
            q = qc * 128 + np.arange(128)
            i = (q - shift) % Tp
            for bi, st in enumerate((s0, s1)):
                k = st + np.arange(512)
                valid = k < Tp
                jj = (k - shift) % Tp
                diff = (mseg[i][:, None] != mseg[jj][None, :]) & valid[None, :]
                am[:, bi * 512:(bi + 1) * 512] = np.where(diff, -100.0, 0.0)
            c[f"amask{br}q{qc}"] = _bf(am)
    return c



STAGE_LIST = [stage_moe, stage_l0attn, stage_moe, stage_l1ffn, stage_l1attn, stage_l1ffn]


def run_stages(x, staged, stage_groups):
    cur = [np.ascontiguousarray(np.asarray(x)[b].T, dtype=np.float32) for b in range(NC)]
    for gi, group in enumerate(stage_groups):
        final = gi == len(stage_groups) - 1
        nc = build_program(group, final_out=final)
        need = set()
        for s in group:
            need.update(s.needs)
        shared = {n: staged[n] for n in need}
        in_maps = [dict(shared, xT=cur[b]) for b in range(NC)]
        res = run_bass_kernel_spmd(nc, in_maps, core_ids=list(range(NC)))
        if final:
            return np.stack([res.results[b]["out"] for b in range(NC)], axis=0)
        cur = [res.results[b]["xout"] for b in range(NC)]
    raise AssertionError


FUSED = True


def kernel(**inputs):
    staged = host_prep(inputs)
    x = np.asarray(inputs["L0_attn_x"], dtype=np.float32)
    groups = [list(STAGE_LIST)] if FUSED else [[s] for s in STAGE_LIST]
    return run_stages(x, staged, groups)


# version 33
# speedup vs baseline: 1.0251x; 1.0251x over previous
import numpy as np
import ml_dtypes
import concourse.bass as bass
import concourse.mybir as mybir
from concourse.bass_utils import run_bass_kernel_spmd

F32 = mybir.dt.float32
BF16 = mybir.dt.bfloat16
AF = mybir.ActivationFunctionType
ALU = mybir.AluOpType
AX = mybir.AxisListType
BF = ml_dtypes.bfloat16

T = 2048
D = 1024
NC = 8

ENGINES = ("pe", "ve", "act", "pool", "sp")
_ENG_ATTR = {"pe": "tensor", "ve": "vector", "act": "scalar", "pool": "gpsimd", "sp": "sync"}


class Op:
    __slots__ = ("eng", "emit", "reads", "writes", "idx", "eidx", "dma_group",
                 "inc_needed", "waits", "tag", "no_self_sync")

    def __init__(self, eng, emit, reads, writes, dma_group=None, tag="", no_self_sync=False):
        self.eng = eng
        self.emit = emit
        self.reads = reads
        self.writes = writes
        self.dma_group = dma_group
        self.inc_needed = dma_group is not None
        self.waits = {}
        self.tag = tag
        self.no_self_sync = no_self_sync


class Prog:

    def __init__(self, nc):
        self.nc = nc
        self.ops = []
        self.last_writer = {}
        self.readers = {}
        self.dma_groups = {}
        self.sems = {}
        self._sem_handles = {}

    def op(self, eng, emit, reads=(), writes=(), dma_group=None, tag="", no_self_sync=False):
        o = Op(eng, emit, list(reads), list(writes), dma_group, tag, no_self_sync)
        o.idx = len(self.ops)
        self.ops.append(o)
        if dma_group is not None:
            self.dma_groups.setdefault(dma_group, []).append(o)
        return o

    def barrier(self):
        self.op(None, None, tag="barrier")

    SELF_SYNC_WAR = False

    def _edge(self, prod, cons, war=False):
        if prod is cons:
            return
        if prod.eng == cons.eng and prod.dma_group is None and cons.dma_group is None:
            if prod.eng == "pe":
                return
            if war and not Prog.SELF_SYNC_WAR:
                return
        cons.waits[prod.idx] = prod
        prod.inc_needed = True

    def finalize(self):
        lw, rd = self.last_writer, self.readers
        barrier_preds = {}
        barrier_todo = set()
        per_eng_last = {}
        per_grp_last = {}
        for o in self.ops:
            if o.tag == "barrier":
                barrier_preds = dict(per_eng_last)
                barrier_preds.update({("g", g): p for g, p in per_grp_last.items()})
                barrier_todo = set(ENGINES)
                continue
            if o.eng in barrier_todo:
                for p in barrier_preds.values():
                    self._edge(p, o)
                barrier_todo.discard(o.eng)
            for r in o.reads:
                p = lw.get(r)
                if p is not None:
                    self._edge(p, o)
                rd.setdefault(r, []).append(o)
            for w in o.writes:
                p = lw.get(w)
                if p is not None:
                    self._edge(p, o)
                for q in rd.get(w, ()):
                    self._edge(q, o, war=True)
                lw[w] = o
                rd[w] = []
            per_eng_last[o.eng] = o
            if o.dma_group is not None:
                per_grp_last[o.dma_group] = o

        eng_ops = {e: [] for e in ENGINES}
        for o in self.ops:
            if o.tag == "barrier":
                continue
            eng_ops[o.eng].append(o)
            o.eidx = len(eng_ops[o.eng]) - 1

        def mksem(name):
            h = self.nc.alloc_semaphore(name=name)
            return h
        eng_sem = {e: mksem(f"s_{e}") for e in ENGINES}
        grp_sem = {g: mksem(f"g_{g}") for g in self.dma_groups}

        inc_count = {}
        cnt = {e: 0 for e in ENGINES}
        gcnt = {g: 0 for g in self.dma_groups}
        for o in self.ops:
            if o.tag == "barrier":
                continue
            if o.dma_group is not None:
                gcnt[o.dma_group] += 16
                inc_count[o.idx] = gcnt[o.dma_group]
            elif o.inc_needed:
                cnt[o.eng] += 1
                inc_count[o.idx] = cnt[o.eng]

        for e in ENGINES:
            high = {}
            for o in eng_ops[e]:
                wlist = []
                for p in o.waits.values():
                    if p.dma_group is not None:
                        sem, val = grp_sem[p.dma_group], inc_count[p.idx]
                    else:
                        if not p.inc_needed:
                            continue
                        sem, val = eng_sem[p.eng], inc_count[p.idx]
                    if high.get(sem, -1) < val:
                        high[sem] = val
                        wlist.append((sem, val))
                o.waits = wlist

        with self.nc.Block() as block:
            for e in ENGINES:
                ops_e = eng_ops[e]
                if not ops_e:
                    continue
                def mk(fn_ops, e=e):
                    def run(eng):
                        for o in fn_ops:
                            for (sem, val) in o.waits:
                                eng.wait_ge(sem, val)
                            inst = o.emit(eng)
                            if o.inc_needed:
                                if o.dma_group is not None:
                                    inst.then_inc(grp_sem[o.dma_group], 16)
                                else:
                                    inst.then_inc(eng_sem[e], 1)
                    return run
                getattr(block, _ENG_ATTR[e])(mk(ops_e))
        return



class Tile:
    __slots__ = ("h", "name")

    def __init__(self, h, name):
        self.h = h
        self.name = name

    def __getitem__(self, key):
        return self.h[key]

    def r(self, key=0):
        return (self.name, key)


class Ctx:
    def __init__(self, nc, P, dram, consts_np):
        self.nc = nc
        self.P = P
        self.dram = dram
        self.uid = 0
        self.scopes = []
        self.pA = nc.alloc_psum_tensor("pA", [128, 2048], F32)
        self.pB = nc.alloc_psum_tensor("pB", [128, 2048], F32)
        self.X = Tile(nc.alloc_sbuf_tensor("X", [128, 8 * T], F32), "X")
        ctx = self

        class _Consts:
            def __init__(self):
                self.loaded = {}

            def __getitem__(self, k):
                if k not in self.loaded:
                    v = consts_np[k]
                    h = nc.inline_tensor(np.ascontiguousarray(v), f"c_{k}")
                    shp = list(v.shape) if v.ndim == 2 else [1, v.shape[0]]
                    t = nc.alloc_sbuf_tensor(f"ct_{k}", shp, BF16 if v.dtype == BF else F32,
                                             side="right")
                    self.loaded[k] = t
                    ctx.dma("sp", t[:], h.ap(), reads=[], writes=[("const", k)], group="cload")
                return self.loaded[k]

        self.const = _Consts()
        self.consts_np = consts_np

    def const_scoped(self, k):
        v = self.consts_np[k]
        h = self.nc.inline_tensor(np.ascontiguousarray(v), self.nm(f"cs_{k}"))
        shp = list(v.shape) if v.ndim == 2 else [1, v.shape[0]]
        t = self.sb(shp, BF16 if v.dtype == BF else F32, f"cst_{k}")
        self.dma("sp", t[:], h.ap(), reads=[], writes=[t.r()], group="cload")
        return t

    def nm(self, s):
        self.uid += 1
        return f"{s}{self.uid}"

    def sb(self, shape, dtype, name=None):
        nm = self.nm(name or "t")
        if self.scopes:
            h = self.scopes[-1].enter_context(self.nc.sbuf_tensor(nm, list(shape), dtype))
        else:
            h = self.nc.alloc_sbuf_tensor(nm, list(shape), dtype)
        return Tile(h, nm)

    from contextlib import contextmanager as _cm

    @_cm
    def scoped(self):
        from contextlib import ExitStack
        st = ExitStack()
        self.scopes.append(st)
        try:
            yield
        finally:
            self.scopes.pop()
            st.close()

    def mm(self, out, lhsT, rhs, reads, wregion, start=True, stop=True, tile_position=None, tag="mm"):
        def e(eng, out=out, lhsT=lhsT, rhs=rhs, start=start, stop=stop, tp=tile_position):
            return eng.matmul(out, lhsT, rhs, start=start, stop=stop, tile_position=tp)
        self.P.op("pe", e, reads=reads, writes=[wregion], tag=tag)

    def mm_acc(self, out, pairs, reads, wregion, tag="mma", tile_position=None):
        n = len(pairs)
        for i, (l, r) in enumerate(pairs):
            self.mm(out, l, r, reads, wregion, start=(i == 0), stop=(i == n - 1),
                    tag=tag, tile_position=tile_position)

    def transp(self, out, src, reads, wregion, dtype_bf=True, tag="tr"):
        ident = self.const["ident"] if dtype_bf else self.const["identf"]
        self.mm(out, src, ident[:], reads, wregion, tag=tag)

    def ve(self, emit, reads, writes, tag="ve"):
        self.P.op("ve", emit, reads, writes, tag=tag)

    def act(self, emit, reads, writes, tag="act"):
        self.P.op("act", emit, reads, writes, tag=tag)

    def pool(self, emit, reads, writes, tag="pool"):
        self.P.op("pool", emit, reads, writes, tag=tag)

    def cp(self, eng, out, in_, reads, writes, tag="cp"):
        if eng == "act":
            self.P.op(eng, lambda e, o=out, i=in_: e.copy(o, i), reads, writes, tag=tag)
        else:
            self.P.op(eng, lambda e, o=out, i=in_: e.tensor_copy(o, i), reads, writes, tag=tag)

    def tt(self, out, in0, in1, op, reads, writes, eng="ve", tag="tt"):
        self.P.op(eng, lambda e, o=out, a=in0, b=in1, p=op: e.tensor_tensor(o, a, b, p),
                  reads, writes, tag=tag)

    def ts(self, out, in0, s1, s2, op0, op1, reads, writes, eng="ve", tag="ts"):
        if s2 is None:
            self.P.op(eng, lambda e, o=out, a=in0, x=s1, p=op0: e.tensor_single_scalar(o, a, x, p),
                      reads, writes, tag=tag)
        else:
            self.P.op(eng, lambda e, o=out, a=in0, x=s1, y=s2, p=op0, q=op1:
                      e.tensor_scalar(o, a, x, y, p, q), reads, writes, tag=tag)

    def stt(self, out, in0, scalar, in1, op0, op1, reads, writes, eng="ve", tag="stt"):
        self.P.op(eng, lambda e, o=out, a=in0, s=scalar, b=in1, p=op0, q=op1:
                  e.scalar_tensor_tensor(o, a, s, b, p, q), reads, writes, tag=tag)

    def actv(self, out, in_, func, reads, writes, bias=0.0, scale=1.0, accum=None, tag="actv"):
        self.P.op("act", lambda e, o=out, i=in_, f=func, b=bias, s=scale, ac=accum:
                  e.activation(o, i, f, bias=b, scale=s, accum_out=ac), reads, writes, tag=tag)

    def red(self, out, in_, op, reads, writes, negate=False, eng="ve", tag="red"):
        self.P.op(eng, lambda e, o=out, i=in_, p=op, n=negate:
                  e.tensor_reduce(o, i, AX.X, p, negate=n), reads, writes, tag=tag)

    def recip(self, out, in_, reads, writes, tag="recip"):
        self.P.op("ve", lambda e, o=out, i=in_: e.reciprocal(o, i), reads, writes, tag=tag)

    def memset(self, ap, val, writes, eng="pool", tag="memset"):
        self.P.op(eng, lambda e, a=ap, v=val: e.memset(a, v), [], writes, tag=tag)

    def dma(self, queue, out, in_, reads, writes, group, tag="dma"):
        self.P.op(queue, lambda e, o=out, i=in_: e.dma_start(o, i),
                  reads, writes, dma_group=group, tag=tag)

    DEBUG = False

    def dbg(self, name, ap, reads):
        if not Ctx.DEBUG:
            return
        shp = list(ap.shape)
        t = self.nc.dram_tensor(f"dbg_{name}", shp, ap.dtype, kind="ExternalOutput").ap()
        self.dma("sp", t, ap, reads=reads, writes=[("dram", f"dbg_{name}")], group="xstore")



def ln_factors(ctx, src, src_regions, nchunks=8, dim=None, want_rms2=False,
               eps_ln=1e-5, eps_rms=1e-6, src_is_bf=False, rms_only=False,
               eps_rms_only=None):
    dim = dim if dim is not None else nchunks * 128
    ones = ctx.const["ones128"]
    onesr = ("const", "ones128")
    out = {}
    B = ctx.sb([128, T], F32, "lnB")
    out["B"] = B
    if not rms_only:
        A = ctx.sb([128, T], F32, "lnA")
        out["A"] = A
    if want_rms2:
        r = ctx.sb([128, T], F32, "lnr")
        out["B2"] = r
    with ctx.scoped():
        sq2 = [ctx.sb([128, T], BF16, "lnsq") for _ in range(2)]
        if not src_is_bf and not rms_only:
            xb2 = [ctx.sb([128, T], BF16, "lnxb") for _ in range(2)]
        for c in range(nchunks):
            sl = slice(c * T, (c + 1) * T)
            s = c % 2
            sq = sq2[s]
            ctx.actv(sq[:], src[:, sl], AF.Square, [src_regions[c]], [sq.r()])
            if not rms_only:
                if src_is_bf:
                    xbt, xread = src, src_regions[c]
                    xsl = sl
                else:
                    xbt, xread = xb2[s], xb2[s].r()
                    xsl = slice(0, T)
                    ctx.cp("ve", xbt[:], src[:, sl], [src_regions[c]], [xread])
            for nb in range(4):
                osl = slice(nb * 512, (nb + 1) * 512)
                if not rms_only:
                    csl = slice(xsl.start + nb * 512, xsl.start + (nb + 1) * 512)
                    ctx.mm(ctx.pA[:, osl], ones[:], xbt[:, csl], [onesr, xread], ("pA", nb),
                           start=(c == 0), stop=(c == nchunks - 1), tag="lnS1")
                ctx.mm(ctx.pB[:, osl], ones[:], sq[:, nb * 512:(nb + 1) * 512], [onesr, sq.r()],
                       ("pB", nb), start=(c == 0), stop=(c == nchunks - 1), tag="lnS2")
        tmp = ctx.sb([128, T], F32, "lntmp")
        pAr = [("pA", nb) for nb in range(4)]
        pBr = [("pB", nb) for nb in range(4)]
        inv_d = 1.0 / dim
        if rms_only:
            ctx.ts(tmp[:], ctx.pB[:, 0:T], inv_d, eps_rms_only, ALU.mult, ALU.add, pBr, [tmp.r()])
        else:
            ctx.ts(A[:], ctx.pA[:, 0:T], inv_d, None, ALU.mult, None, pAr, [A.r()])
            ctx.tt(tmp[:], A[:], A[:], ALU.mult, [A.r()], [tmp.r()])
            ctx.ts(B[:], ctx.pB[:, 0:T], inv_d, eps_ln, ALU.mult, ALU.add, pBr, [B.r()])
            ctx.tt(tmp[:], B[:], tmp[:], ALU.subtract, [B.r(), tmp.r()], [tmp.r()])
        if want_rms2:
            ctx.recip(r[:], tmp[:], [tmp.r()], [r.r()])
            ctx.ts(r[:], r[:], -eps_ln, 1.0 + eps_rms, ALU.mult, ALU.add, [r.r()], [r.r()])
            ctx.actv(r[:], r[:], AF.Ln, [r.r()], [r.r()])
            ctx.actv(r[:], r[:], AF.Exp, [r.r()], [r.r()], scale=-0.5)
        ctx.actv(B[:], tmp[:], AF.Ln, [tmp.r()], [B.r()])
        ctx.actv(B[:], B[:], AF.Exp, [B.r()], [B.r()], scale=-0.5)
    ctx.P.barrier()
    return out



def load_pc(ctx, name, C, dtype=F32, queue="sp"):
    t = ctx.sb([128, C], dtype, f"pc_{name}")
    ctx.dma(queue if dtype == F32 else "pool", t[:], ctx.dram[name],
            reads=[], writes=[t.r()], group="small")
    return t


def stage_moe(ctx):
    d = ctx.dram
    X = ctx.X
    Xr = [X.r(c) for c in range(8)]
    with ctx.scoped():
        xn = ctx.sb([128, 8 * T], BF16, "xn")
        with ctx.scoped():
            f = ln_factors(ctx, X, Xr, want_rms2=True)
            A, B, B2 = f["A"], f["B"], f["B2"]
            Bc = ctx.sb([128, T], F32, "Bc")
            ctx.tt(Bc[:], B[:], B2[:], ALU.mult, [B.r(), B2.r()], [Bc.r()])
            scl = load_pc(ctx, "moe_scale_pc", 8)
            t1 = ctx.sb([128, T], F32, "t1")
            u1 = ctx.sb([128, T], F32, "u1")
            for c in range(8):
                sl = slice(c * T, (c + 1) * T)
                ctx.tt(t1[:], X[:, sl], A[:], ALU.subtract, [Xr[c], A.r()], [t1.r()])
                ctx.stt(xn[:, sl], t1[:], scl[:, c:c + 1], Bc[:], ALU.mult, ALU.mult,
                        [t1.r(), scl.r(), Bc.r()], [xn.r(c)])
                ctx.stt(u1[:], t1[:], 0.5, B[:], ALU.mult, ALU.mult, [t1.r(), B.r()], [u1.r()])
                ctx.tt(X[:, sl], X[:, sl], u1[:], ALU.add, [Xr[c], u1.r()], [Xr[c]])
        ctx.P.barrier()
        xnr = [xn.r(c) for c in range(8)]

        routerT = ctx.sb([128, 8 * 8], BF16, "routerT")
        ctx.dma("pool", routerT[:].rearrange("p (c n) -> p c n", n=8),
                d["moe_routerT"].rearrange("(c p) n -> p c n", p=128),
                reads=[], writes=[routerT.r()], group="small2")
        for tc in range(16):
            ctx.mm_acc(ctx.pA[:, tc * 8:(tc + 1) * 8],
                       [(xn[:, c * T + tc * 128: c * T + (tc + 1) * 128], routerT[:, c * 8:(c + 1) * 8])
                        for c in range(8)],
                       reads=xnr + [routerT.r()], wregion=("pA", 0), tag="router")
        s8 = ctx.sb([128, 128], F32, "s8")
        ctx.cp("ve", s8[:], ctx.pA[:, 0:128], [("pA", 0)], [s8.r()])

        w8 = ctx.sb([128, 128], F32, "w8")
        m1 = ctx.sb([128, 16], F32, "m1")
        m2 = ctx.sb([128, 16], F32, "m2")
        oh = ctx.sb([128, 128], F32, "oh")
        msk = ctx.sb([128, 128], F32, "msk")
        wgt = ctx.sb([128, 16 * 2], F32, "wgt")
        for g in range(16):
            gs = slice(g * 8, (g + 1) * 8)
            gc = slice(g, g + 1)
            ctx.red(m1[:, gc], s8[:, gs], ALU.max, [s8.r()], [m1.r(g)])
            ctx.ts(oh[:, gs], s8[:, gs], m1[:, gc], None, ALU.is_ge, None,
                   [s8.r(), m1.r(g)], [oh.r(g)])
            ctx.stt(msk[:, gs], oh[:, gs], -1e30, s8[:, gs], ALU.mult, ALU.add,
                    [oh.r(g), s8.r()], [msk.r(g)])
            ctx.red(m2[:, gc], msk[:, gs], ALU.max, [msk.r(g)], [m2.r(g)])
        ctx.tt(wgt[:, 0:16], m2[:], m1[:], ALU.subtract,
               [m2.r(g) for g in range(16)] + [m1.r(g) for g in range(16)], [wgt.r("d")])
        ctx.actv(wgt[:, 0:16], wgt[:, 0:16], AF.Sigmoid, [wgt.r("d")], [wgt.r("d")])
        ctx.ts(wgt[:, 0:16], wgt[:, 0:16], 0.5, None, ALU.mult, None, [wgt.r("d")], [wgt.r("d")])
        ctx.ts(wgt[:, 16:32], wgt[:, 0:16], -1.0, 0.5, ALU.mult, ALU.add,
               [wgt.r("d")], [wgt.r("w1")])
        for g in range(16):
            gs = slice(g * 8, (g + 1) * 8)
            ctx.ts(msk[:, gs], msk[:, gs], m2[:, g:g + 1], None, ALU.is_ge, None,
                   [msk.r(g), m2.r(g)], [msk.r(g)])
            ctx.ts(oh[:, gs], oh[:, gs], wgt[:, 16 + g:17 + g], None, ALU.mult, None,
                   [oh.r(g), wgt.r("w1")], [oh.r(g)])
            ctx.stt(w8[:, gs], msk[:, gs], wgt[:, g:g + 1], oh[:, gs], ALU.mult, ALU.add,
                    [msk.r(g), wgt.r("d"), oh.r(g)], [w8.r(g)])
        w8r = [w8.r(g) for g in range(16)]
        ctx.mm(ctx.pB[:, 0:128], w8[:], ctx.const["identf"][:], w8r + [("const", "identf")],
               ("pB", 0), tag="w8T")
        w8T = ctx.sb([128, 128], F32, "w8T")
        ctx.cp("ve", w8T[:], ctx.pB[:, 0:128], [("pB", 0)], [w8T.r()])
        scname = ctx.nm("w8scr")
        scr = ctx.nc.dram_tensor(scname, [128, 128], F32, kind="Internal").ap()
        ctx.dma("sp", scr, w8T[:], reads=[w8T.r()], writes=[("dram", scname)], group="w8io")

        wb = ctx.sb([128, 16 * 128], F32, "wb")
        hbuf = ctx.sb([128, 16 * T], BF16, "hbuf")
        sgt = ctx.sb([128, T], BF16, "sgt")
        uwt = ctx.sb([128, T], BF16, "uwt")
        wu = [ctx.sb([128, 2 * 1024], BF16, "wu") for _ in range(2)]
        wd = [ctx.sb([128, 16 * 128], BF16, "wd") for _ in range(2)]
        for e in range(8):
            win = scr.rearrange("(g e) q -> e g q", e=8)[e]
            win = win.unsqueeze(0).broadcast_to([128, 16, 128])
            ctx.dma("sp", wb[:].rearrange("p (g q) -> p g q", g=16), win,
                    reads=[("dram", scname)], writes=[wb.r()], group="wbload")
            for i in range(16):
                s = i % 2
                m_u, m_g = i * 128, (16 + i) * 128
                ctx.dma("pool", wu[s][:, 0:1024].rearrange("p (k m) -> p k m", m=128),
                        d["moe_upT"][e, i],
                        reads=[], writes=[wu[s].r()], group=f"wu{s}")
                ctx.dma("pool", wu[s][:, 1024:2048].rearrange("p (k m) -> p k m", m=128),
                        d["moe_upT"][e, 16 + i],
                        reads=[], writes=[wu[s].r()], group=f"wu{s}")
                for nb in range(4):
                    nsl = slice(nb * 512, (nb + 1) * 512)
                    ctx.mm_acc(ctx.pA[:, nsl],
                               [(wu[s][:, kc * 128:(kc + 1) * 128],
                                 xn[:, kc * T + nb * 512: kc * T + (nb + 1) * 512]) for kc in range(8)],
                               reads=[wu[s].r()] + xnr, wregion=("pA", nb), tag="up_u")
                    ctx.mm_acc(ctx.pB[:, nsl],
                               [(wu[s][:, 1024 + kc * 128: 1024 + (kc + 1) * 128],
                                 xn[:, kc * T + nb * 512: kc * T + (nb + 1) * 512]) for kc in range(8)],
                               reads=[wu[s].r()] + xnr, wregion=("pB", nb), tag="up_g")
                pAr = [("pA", nb) for nb in range(4)]
                pBr = [("pB", nb) for nb in range(4)]
                ctx.actv(sgt[:], ctx.pB[:, 0:T], AF.Silu, pBr, [sgt.r()])
                ctx.tt(uwt[:], ctx.pA[:, 0:T], wb[:, 0:T], ALU.mult, pAr + [wb.r()], [uwt.r()])
                ctx.tt(hbuf[:, i * T:(i + 1) * T], uwt[:], sgt[:], ALU.mult,
                       [uwt.r(), sgt.r()], [hbuf.r(i)])
            for do in range(8):
                s = do % 2
                ctx.dma("pool", wd[s][:].rearrange("p (k m) -> p k m", m=128),
                        d["moe_downT"][e, do],
                        reads=[], writes=[wd[s].r()], group=f"wd{s}")
                for nb in range(4):
                    nsl = slice(nb * 512, (nb + 1) * 512)
                    ctx.mm_acc(ctx.pA[:, nsl],
                               [(wd[s][:, kc * 128:(kc + 1) * 128],
                                 hbuf[:, kc * T + nb * 512: kc * T + (nb + 1) * 512]) for kc in range(16)],
                               reads=[wd[s].r()] + [hbuf.r(i) for i in range(16)],
                               wregion=("pA", nb), tag="down")
                    xsl = slice(do * T + nb * 512, do * T + (nb + 1) * 512)
                    ctx.tt(X[:, xsl], X[:, xsl], ctx.pA[:, nsl], ALU.add,
                           [X.r(do), ("pA", nb)], [X.r(do)])


stage_moe.needs = ["moe_scale_pc", "moe_routerT", "moe_upT", "moe_downT"]



def stage_l1attn(ctx):
    d = ctx.dram
    X = ctx.X
    Xr = [X.r(c) for c in range(8)]
    with ctx.scoped():
        xb = ctx.sb([128, 8 * T], BF16, "xb")
        with ctx.scoped():
            f = ln_factors(ctx, X, Xr)
            A, B = f["A"], f["B"]
            t1 = ctx.sb([128, T], F32, "t1")
            t1b = ctx.sb([128, T], F32, "t1b")
            u1 = ctx.sb([128, T], F32, "u1")
            u1b = ctx.sb([128, T], F32, "u1b")
            for c in range(8):
                sl = slice(c * T, (c + 1) * T)
                eng = "ve"
                tl = t1 if c % 2 == 0 else t1b
                ul = u1 if c % 2 == 0 else u1b
                ctx.tt(tl[:], X[:, sl], A[:], ALU.subtract, [Xr[c], A.r()], [tl.r()], eng=eng)
                ctx.tt(xb[:, sl], tl[:], B[:], ALU.mult, [tl.r(), B.r()], [xb.r(c)], eng=eng)
                ctx.tt(ul[:], tl[:], B[:], ALU.mult, [tl.r(), B.r()], [ul.r()], eng=eng)
                ctx.tt(X[:, sl], X[:, sl], ul[:], ALU.add, [Xr[c], ul.r()], [Xr[c]], eng=eng)
        ctx.P.barrier()
        xbr = [xb.r(c) for c in range(8)]

        bq = load_pc(ctx, "a1_bqkv_pc", 24)
        bo = load_pc(ctx, "a1_bo_pc", 8)
        bvr = ctx.sb([1, 1024], F32, "bvrow")
        ctx.dma("sp", bvr[:], d["a1_bv_row"], reads=[], writes=[bvr.r()], group="small")
        bvb = ctx.sb([128, 1024], F32, "bvb")
        for half in range(2):
            ctx.mm(ctx.pB[:, half * 512:(half + 1) * 512], ctx.const["onescol_f"][:],
                   bvr[:, half * 512:(half + 1) * 512], [("const", "onescol_f"), bvr.r()],
                   ("pB", half), tag="bvb")
            ctx.cp("ve", bvb[:, half * 512:(half + 1) * 512], ctx.pB[:, half * 512:(half + 1) * 512],
                   [("pB", half)], [bvb.r()])

        attn_out = ctx.sb([128, 8 * T], BF16, "aout")
        qt = ctx.sb([128, T], BF16, "qt")
        kt = ctx.sb([128, T], BF16, "kt")
        vt = ctx.sb([128, T], BF16, "vt")
        NRING = 3
        NRT = 2
        pt = [ctx.sb([128, T], BF16, "pt") for _ in range(NRING)]
        ptT = [ctx.sb([128, T], BF16, "ptT") for _ in range(NRT)]
        mx = [ctx.sb([128, 1], F32, "mx") for _ in range(NRING)]
        den = [ctx.sb([128, 1], F32, "den") for _ in range(NRING)]
        wsl = [ctx.sb([128, 3 * 1024], BF16, "wqkv") for _ in range(2)]
        for hc in range(8):
            s = hc % 2
            for part in range(3):
                ctx.dma("pool", wsl[s][:, part * 1024:(part + 1) * 1024].rearrange("p (k m) -> p k m", m=128),
                        d["a1_qkvT"][part * 8 + hc],
                        reads=[], writes=[wsl[s].r()], group=f"qkv{s}")
            wr = [wsl[s].r()]
            for nb in range(4):
                nsl = slice(nb * 512, (nb + 1) * 512)
                ctx.mm_acc(ctx.pA[:, nsl],
                           [(wsl[s][:, kc * 128:(kc + 1) * 128],
                             xb[:, kc * T + nb * 512: kc * T + (nb + 1) * 512]) for kc in range(8)],
                           reads=wr + xbr, wregion=("pA", nb), tag="q")
                ctx.mm_acc(ctx.pB[:, nsl],
                           [(wsl[s][:, 1024 + kc * 128: 1024 + (kc + 1) * 128],
                             xb[:, kc * T + nb * 512: kc * T + (nb + 1) * 512]) for kc in range(8)],
                           reads=wr + xbr, wregion=("pB", nb), tag="k")
            pAr = [("pA", nb) for nb in range(4)]
            pBr = [("pB", nb) for nb in range(4)]
            ctx.ts(qt[:], ctx.pA[:, 0:T], bq[:, hc:hc + 1], 1.0 / 32.0, ALU.add, ALU.mult,
                   pAr + [bq.r()], [qt.r()])
            ctx.ts(kt[:], ctx.pB[:, 0:T], bq[:, 8 + hc:9 + hc], None, ALU.add, None,
                   pBr + [bq.r()], [kt.r()])
            for tc in range(16):
                bank = tc % 4
                psl = slice(bank * 512, bank * 512 + 128)
                ctx.mm_acc(ctx.pA[:, psl],
                           [(xb[:, kc * T + tc * 128: kc * T + (tc + 1) * 128],
                             wsl[s][:, 2048 + kc * 128: 2048 + (kc + 1) * 128]) for kc in range(8)],
                           reads=wr + xbr, wregion=("pA", bank), tag="v")
                ctx.tt(vt[:, tc * 128:(tc + 1) * 128], ctx.pA[:, psl],
                       bvb[:, hc * 128:(hc + 1) * 128], ALU.add,
                       [("pA", bank), bvb.r()], [vt.r(tc)], tag="vevac")
            vtr = [vt.r(tc) for tc in range(16)]
            av_sl = slice(3 * 512, 3 * 512 + 128)
            groups = [(qc, j) for qc in range(16) for j in range(2)]

            def front(g):
                qc, j = groups[g]
                b0 = 64 * j
                rp = g % NRING
                for nb in range(4):
                    ctx.mm(ctx.pB[:, nb * 512:(nb + 1) * 512],
                           qt[b0:b0 + 64, qc * 128:(qc + 1) * 128],
                           kt[b0:b0 + 64, nb * 512:(nb + 1) * 512],
                           [qt.r(), kt.r()], ("pB", nb), tag="sc")
                ctx.red(mx[rp][:], ctx.pB[:, 0:T], ALU.max, pBr, [mx[rp].r()], negate=True)
                ctx.actv(pt[rp][:], ctx.pB[:, 0:T], AF.Exp, pBr + [mx[rp].r()],
                         [pt[rp].r(), den[rp].r()], bias=mx[rp][:], accum=den[rp][:])
                ctx.recip(den[rp][:], den[rp][:], [den[rp].r()], [den[rp].r()])
                ctx.ts(pt[rp][:], pt[rp][:], den[rp][:], None, ALU.mult, None,
                       [pt[rp].r(), den[rp].r()], [pt[rp].r()])

            def back_tr(g):
                rp = g % NRING
                rt = g % NRT
                for batch in range(4):
                    bank = (g * 4 + batch) % 3
                    for jc4 in range(4):
                        jc = batch * 4 + jc4
                        psl = slice(bank * 512 + jc4 * 128, bank * 512 + (jc4 + 1) * 128)
                        ctx.mm(ctx.pA[:, psl], pt[rp][:, jc * 128:(jc + 1) * 128],
                               ctx.const["ident"][:], [pt[rp].r(), ("const", "ident")],
                               ("pA", bank), tag="ptT")
                    ctx.cp("act" if batch % 2 == 0 else "ve",
                           ptT[rt][:, batch * 512:(batch + 1) * 512],
                           ctx.pA[:, bank * 512:(bank + 1) * 512],
                           [("pA", bank)], [ptT[rt].r(batch)])

            def back_av(g):
                qc, j = groups[g]
                b0 = 64 * j
                rt = g % NRT
                ctx.mm_acc(ctx.pA[b0:b0 + 64, av_sl],
                           [(vt[:, jc * 128 + b0: jc * 128 + b0 + 64],
                             ptT[rt][:, jc * 128:(jc + 1) * 128]) for jc in range(16)],
                           reads=vtr + [ptT[rt].r(b) for b in range(4)],
                           wregion=("pA", 3), tag="av", tile_position=(0, b0))
                if j == 1:
                    ctx.cp("act", attn_out[:, hc * T + qc * 128: hc * T + (qc + 1) * 128],
                           ctx.pA[:, av_sl], [("pA", 3)], [attn_out.r(hc)])

            NG = len(groups)
            for gi in range(NG + 3):
                if gi >= 3:
                    back_av(gi - 3)
                if 2 <= gi < NG + 2:
                    back_tr(gi - 2)
                if gi < NG:
                    front(gi)
        wo = [ctx.sb([128, 1024], BF16, "wo") for _ in range(2)]
        aor = [attn_out.r(c) for c in range(8)]
        for oc in range(8):
            s = oc % 2
            ps = ctx.pA if oc % 2 == 0 else ctx.pB
            pn = "pA" if oc % 2 == 0 else "pB"
            ctx.dma("pool", wo[s][:].rearrange("p (k m) -> p k m", m=128),
                    d["a1_woT"][oc],
                    reads=[], writes=[wo[s].r()], group=f"wo{s}")
            for nb in range(4):
                nsl = slice(nb * 512, (nb + 1) * 512)
                ctx.mm_acc(ps[:, nsl],
                           [(wo[s][:, kc * 128:(kc + 1) * 128],
                             attn_out[:, kc * T + nb * 512: kc * T + (nb + 1) * 512]) for kc in range(8)],
                           reads=[wo[s].r()] + aor, wregion=(pn, nb), tag="woP")
                xsl = slice(oc * T + nb * 512, oc * T + (nb + 1) * 512)
                ctx.stt(X[:, xsl], ps[:, nsl], bo[:, oc:oc + 1], X[:, xsl],
                        ALU.add, ALU.add, [(pn, nb), bo.r(), X.r(oc)], [X.r(oc)])


stage_l1attn.needs = ["a1_qkvT", "a1_bqkv_pc", "a1_bv_row", "a1_woT", "a1_bo_pc"]



L0_TP = (2050, 2060)
L0_L = 2176


def _l0_branch(ctx, br, xin, xinr, pout, poutr):
    d = ctx.dram
    L = L0_L
    nL = L // 128
    with ctx.scoped():
        xpad = ctx.sb([128, 2 * L], BF16, "xpad")
        for cc in range(2):
            ctx.cp("ve", xpad[:, cc * L: cc * L + T], xin[:, (2 * br + cc) * T:(2 * br + cc + 1) * T],
                   [xinr[2 * br + cc]], [xpad.r(cc)])
            ctx.memset(xpad[:, cc * L + T: (cc + 1) * L], 0.0, [xpad.r(cc)], eng="pool")
        xpr = [xpad.r(0), xpad.r(1)]
        qb = ctx.sb([128, 2 * L], BF16, "qb")
        kb = ctx.sb([128, 2 * L], BF16, "kb")
        vt = ctx.sb([128, nL * 256], BF16, "vt")
        qkvb = load_pc(ctx, "a0_qkvb_pc", 12)
        wq = ctx.sb([128, 512], BF16, "wq")
        bvr = ctx.sb([1, 256], F32, "bvr")
        ctx.dma("sp", bvr[:], d["a0_bv"][br], reads=[], writes=[bvr.r()], group="small")
        bvb = ctx.sb([128, 256], F32, "bvb")
        ctx.mm(ctx.pB[:, 0:256], ctx.const["onescol_f"][:], bvr[:],
               [("const", "onescol_f"), bvr.r()], ("pB", 0), tag="bvb0")
        ctx.cp("ve", bvb[:], ctx.pB[:, 0:256], [("pB", 0)], [bvb.r()])

        nb5 = [(0, 512), (512, 512), (1024, 512), (1536, 512), (2048, 128)]
        for part, (dst, dstr) in enumerate(((qb, "q"), (kb, "k"))):
            ctx.dma("pool", wq[:].rearrange("p (k m) -> p k m", m=256),
                    d["a0_qkv_w"][br][:, part * 256:(part + 1) * 256]
                    .rearrange("(k p) m -> p k m", p=128),
                    reads=[], writes=[wq.r()], group="a0w")
            for hg in range(2):
                for (n0, nn) in nb5:
                    po = n0 if n0 < 2048 else 0
                    ctx.mm_acc(ctx.pA[:, po:po + nn],
                               [(wq[:, kc * 256 + hg * 128: kc * 256 + hg * 128 + 128],
                                 xpad[:, cc * L + n0: cc * L + n0 + nn])
                                for kc, cc in ((0, 0), (1, 1))],
                               reads=[wq.r()] + xpr, wregion=("pA", po // 512),
                               tag="a0qk")
                    bcol = qkvb[:, br * 6 + part * 2 + hg: br * 6 + part * 2 + hg + 1]
                    ctx.ts(dst[:, hg * L + n0: hg * L + n0 + nn], ctx.pA[:, po:po + nn],
                           bcol, (1.0 / np.sqrt(32.0)) if part == 0 else None,
                           ALU.add, ALU.mult if part == 0 else None,
                           [("pA", po // 512), qkvb.r()], [dst.r((dstr, hg))])
        ctx.dma("pool", wq[:].rearrange("p (k m) -> p k m", m=256),
                d["a0_qkv_w"][br][:, 512:768].rearrange("(k p) m -> p k m", p=128),
                reads=[], writes=[wq.r()], group="a0w")
        for tc in range(nL):
            bank = tc % 4
            psl = slice(bank * 512, bank * 512 + 256)
            ctx.mm_acc(ctx.pA[:, psl],
                       [(xpad[:, kc2 * L + tc * 128: kc2 * L + tc * 128 + 128],
                         wq[:, kc2 * 256: kc2 * 256 + 256]) for kc2 in range(2)],
                       reads=[wq.r()] + xpr, wregion=("pA", bank), tag="a0v")
            ctx.tt(vt[:, tc * 256:(tc + 1) * 256], ctx.pA[:, psl], bvb[:], ALU.add,
                   [("pA", bank), bvb.r()], [vt.r(tc)])
        vtr = [vt.r(tc) for tc in range(nL)]
        qbr = [qb.r(("q", 0)), qb.r(("q", 1))]
        kbr = [kb.r(("k", 0)), kb.r(("k", 1))]

        NR0 = 3
        NRT0 = 2
        bias_sb = [ctx.sb([128, 1024], BF16, "a0bias") for _ in range(3)]
        po = [ctx.sb([128, 1024], BF16, "po") for _ in range(NR0)]
        poT = [ctx.sb([128, 1024], BF16, "poT") for _ in range(NRT0)]
        mx = [ctx.sb([128, 1], F32, "mx") for _ in range(NR0)]
        den = [ctx.sb([128, 1], F32, "den") for _ in range(NR0)]
        bout = ctx.sb([128, 2 * T], BF16, "bout")
        av_sl = slice(3 * 512, 3 * 512 + 128)
        amask_tiles = {}
        groups = [(qc, h) for qc in range(16) for h in range(8)]

        def meta(g):
            qc, h = groups[g]
            s0 = min(max((qc - 1) * 128, 0), L - 512)
            s1 = 1664 if qc != 15 else 0
            nband = 2 if qc in (0, 15) else 1
            return qc, h, s0, s1, nband

        def front(g):
            qc, h, s0, s1, nband = meta(g)
            NW = nband * 512
            rp = g % NR0
            sb_ = g % 3
            b0 = 32 * (h % 4)
            hg = h // 4
            amask_t = None
            if (br == 0 and qc in (0, 15)) or (br == 1 and qc == 0):
                if qc not in amask_tiles:
                    amask_tiles[qc] = ctx.const_scoped(f"amask{br}q{qc}")
                amask_t = amask_tiles[qc]
            ctx.dma("sp", bias_sb[sb_][:, 0:NW], d["a0_bias"][br, h, qc, :, 0:NW],
                    reads=[], writes=[bias_sb[sb_].r()], group=f"a0b{sb_}")
            for bi, bst in enumerate((s0, s1)[:nband]):
                ctx.mm(ctx.pB[:, bi * 512:(bi + 1) * 512],
                       qb[b0:b0 + 32, hg * L + qc * 128: hg * L + (qc + 1) * 128],
                       kb[b0:b0 + 32, hg * L + bst: hg * L + bst + 512],
                       qbr + kbr, ("pB", bi), start=True, stop=False,
                       tag="a0sc", tile_position=(b0, 0))
                last = amask_t is None
                ctx.mm(ctx.pB[:, bi * 512:(bi + 1) * 512], ctx.const["ident"][:],
                       bias_sb[sb_][:, bi * 512:(bi + 1) * 512],
                       [("const", "ident"), bias_sb[sb_].r()], ("pB", bi),
                       start=False, stop=last, tag="a0bi")
                if amask_t is not None:
                    ctx.mm(ctx.pB[:, bi * 512:(bi + 1) * 512], ctx.const["ident"][:],
                           amask_t[:, bi * 512:(bi + 1) * 512],
                           [("const", "ident"), amask_t.r()], ("pB", bi),
                           start=False, stop=True, tag="a0am")
            pBb = [("pB", bi) for bi in range(nband)]
            ctx.red(mx[rp][:], ctx.pB[:, 0:NW], ALU.max, pBb, [mx[rp].r()], negate=True)
            ctx.actv(po[rp][:, 0:NW], ctx.pB[:, 0:NW], AF.Exp, pBb + [mx[rp].r()],
                     [po[rp].r(), den[rp].r()], bias=mx[rp][:], accum=den[rp][:])
            ctx.recip(den[rp][:], den[rp][:], [den[rp].r()], [den[rp].r()])
            ctx.ts(po[rp][:, 0:NW], po[rp][:, 0:NW], den[rp][:], None, ALU.mult, None,
                   [po[rp].r(), den[rp].r()], [po[rp].r()])

        def back_tr(g):
            qc, h, s0, s1, nband = meta(g)
            rp = g % NR0
            rt = g % NRT0
            for batch in range(nband):
                bank = (g + batch) % 3
                for jc4 in range(4):
                    jc = batch * 4 + jc4
                    psl = slice(bank * 512 + jc4 * 128, bank * 512 + (jc4 + 1) * 128)
                    ctx.mm(ctx.pA[:, psl], po[rp][:, jc * 128:(jc + 1) * 128],
                           ctx.const["ident"][:], [po[rp].r(), ("const", "ident")],
                           ("pA", bank), tag="a0pT")
                ctx.cp("act" if (g + batch) % 2 == 0 else "ve",
                       poT[rt][:, batch * 512:(batch + 1) * 512],
                       ctx.pA[:, bank * 512:(bank + 1) * 512],
                       [("pA", bank)], [poT[rt].r(batch)])

        def back_av(g):
            qc, h, s0, s1, nband = meta(g)
            rt = g % NRT0
            b0 = 32 * (h % 4)
            hg = h // 4
            pairs = []
            for jc in range(4 * nband):
                bst = s0 if jc < 4 else s1
                kchunk = bst // 128 + (jc % 4)
                pairs.append((vt[:, kchunk * 256 + 32 * h: kchunk * 256 + 32 * h + 32],
                              poT[rt][:, jc * 128:(jc + 1) * 128]))
            ctx.mm_acc(ctx.pA[b0:b0 + 32, av_sl], pairs,
                       reads=vtr + [poT[rt].r(b) for b in range(nband)],
                       wregion=("pA", 3), tag="a0av", tile_position=(0, b0))
            if h % 4 == 3:
                ctx.cp("act", bout[:, hg * T + qc * 128: hg * T + (qc + 1) * 128],
                       ctx.pA[:, av_sl], [("pA", 3)], [bout.r(hg)])

        NGr = len(groups)
        for gi in range(NGr + 3):
            if gi >= 3:
                back_av(gi - 3)
            if 2 <= gi < NGr + 2:
                back_tr(gi - 2)
            if gi < NGr:
                front(gi)
        if br == 0:
            ctx.dbg("qb", qb[:], qbr)
            ctx.dbg("kb", kb[:], kbr)
            ctx.dbg("vt", vt[:], vtr)
            ctx.dbg("bout", bout[:], [bout.r(0), bout.r(1)])
        projb = load_pc(ctx, "a0_projb_pc", 4)
        wp = ctx.sb([128, 256], BF16, "wp")
        for oc in range(2):
            ctx.dma("pool", wp[:].rearrange("p (k m) -> p k m", m=128),
                    d["a0_proj_w"][br][:, oc * 128:(oc + 1) * 128]
                    .rearrange("(k p) m -> p k m", p=128),
                    reads=[], writes=[wp.r()], group="a0w")
            for nb in range(4):
                nsl = slice(nb * 512, (nb + 1) * 512)
                ctx.mm_acc(ctx.pA[:, nsl],
                           [(wp[:, kc * 128:(kc + 1) * 128],
                             bout[:, kc * T + nb * 512: kc * T + (nb + 1) * 512]) for kc in range(2)],
                           reads=[wp.r(), bout.r(0), bout.r(1)], wregion=("pA", nb), tag="a0pr")
                oci = 2 * br + oc
                ctx.ts(pout[:, oci * T + nb * 512: oci * T + (nb + 1) * 512], ctx.pA[:, nsl],
                       projb[:, br * 2 + oc: br * 2 + oc + 1], None, ALU.add, None,
                       [("pA", nb), projb.r()], [poutr[oci]])


def stage_l0attn(ctx):
    d = ctx.dram
    X = ctx.X
    Xr = [X.r(c) for c in range(8)]
    with ctx.scoped():
        f = ln_factors(ctx, X, Xr)
        A, B = f["A"], f["B"]
        pout = ctx.sb([128, 4 * T], BF16, "pout")
        poutr = [pout.r(c) for c in range(4)]
        with ctx.scoped():
            xin = ctx.sb([128, 4 * T], BF16, "xin")
            xinr = [xin.r(c) for c in range(4)]
            with ctx.scoped():
                downb = load_pc(ctx, "a0_downb_pc", 4)
                n1w = load_pc(ctx, "a0_n1w_pc", 4)
                wd = ctx.sb([128, 8 * 512], BF16, "wdn")
                t1 = ctx.sb([128, 1024], F32, "dt1")
                xa = ctx.sb([128, 1024], BF16, "dxa")
                sqb = ctx.sb([128, T], BF16, "dsq")
                fac = ctx.sb([128, T], F32, "dfac")
                ctx.dma("pool", wd[:].rearrange("p (k m) -> p k m", m=512),
                        d["a0_down_w"].rearrange("(k p) m -> p k m", p=128),
                        reads=[], writes=[wd.r()], group="a0w")
                pr = {0: ("pA", 0), 1: ("pA", 2), 2: ("pB", 0), 3: ("pB", 2)}
                prr = {0: [("pA", 0), ("pA", 1)], 1: [("pA", 2), ("pA", 3)],
                       2: [("pB", 0), ("pB", 1)], 3: [("pB", 2), ("pB", 3)]}
                pap = {0: ctx.pA, 1: ctx.pA, 2: ctx.pB, 3: ctx.pB}
                pof = {0: 0, 1: 1024, 2: 0, 3: 1024}
                for th in range(2):
                    t0 = th * 1024
                    for kc in range(8):
                        ctx.tt(t1[:], X[:, kc * T + t0: kc * T + t0 + 1024], A[:, t0:t0 + 1024],
                               ALU.subtract, [Xr[kc], A.r()], [t1.r()])
                        ctx.tt(xa[:], t1[:], B[:, t0:t0 + 1024], ALU.mult,
                               [t1.r(), B.r()], [xa.r()])
                        for oc in range(4):
                            for n2 in range(2):
                                po = pof[oc] + n2 * 512
                                ctx.mm(pap[oc][:, po:po + 512],
                                       wd[:, kc * 512 + oc * 128: kc * 512 + (oc + 1) * 128],
                                       xa[:, n2 * 512:(n2 + 1) * 512],
                                       [wd.r(), xa.r()], prr[oc][n2],
                                       start=(kc == 0), stop=(kc == 7), tag="a0dn")
                    for oc in range(4):
                        ctx.ts(xin[:, oc * T + t0: oc * T + t0 + 1024],
                               pap[oc][:, pof[oc]:pof[oc] + 1024], downb[:, oc:oc + 1], None,
                               ALU.add, None, prr[oc] + [downb.r()], [xinr[oc]])
                for oc in range(4):
                    ctx.actv(sqb[:], xin[:, oc * T:(oc + 1) * T], AF.Square,
                             [xinr[oc]], [sqb.r()])
                    for nb in range(4):
                        ctx.mm(ctx.pB[:, nb * 512:(nb + 1) * 512], ctx.const["ones128"][:],
                               sqb[:, nb * 512:(nb + 1) * 512],
                               [("const", "ones128"), sqb.r()], ("pB", nb),
                               start=(oc == 0), stop=(oc == 3), tag="a0S2")
                pBr = [("pB", nb) for nb in range(4)]
                ctx.ts(fac[:], ctx.pB[:, 0:T], 1.0 / 512, 1e-6, ALU.mult, ALU.add, pBr, [fac.r()])
                ctx.actv(fac[:], fac[:], AF.Ln, [fac.r()], [fac.r()])
                ctx.actv(fac[:], fac[:], AF.Exp, [fac.r()], [fac.r()], scale=-0.5)
                for oc in range(4):
                    sl = slice(oc * T, (oc + 1) * T)
                    ctx.stt(xin[:, sl], xin[:, sl], n1w[:, oc:oc + 1], fac[:], ALU.mult, ALU.mult,
                            [xinr[oc], n1w.r(), fac.r()], [xinr[oc]])
                ctx.dbg("xin", xin[:], xinr)
            ctx.P.barrier()
            _l0_branch(ctx, 0, xin, xinr, pout, poutr)
            ctx.P.barrier()
            _l0_branch(ctx, 1, xin, xinr, pout, poutr)
            ctx.P.barrier()
        ctx.dbg("pout", pout[:], poutr)
        ctx.P.barrier()
        xint = ctx.sb([128, 8 * T], BF16, "xint")
        xir = [xint.r(c) for c in range(8)]
        with ctx.scoped():
            upb = load_pc(ctx, "a0_upb_pc", 8)
            wu = ctx.sb([128, 512], BF16, "wup")
            t1 = ctx.sb([128, T], F32, "ut1")
            t2 = ctx.sb([128, T], F32, "ut2")
            tu = ctx.sb([128, 512], F32, "utu")
            for oc in range(8):
                ctx.dma("pool", wu[:].rearrange("p (k m) -> p k m", m=128),
                        d["a0_up_w"][:, oc * 128:(oc + 1) * 128].rearrange("(k p) m -> p k m", p=128),
                        reads=[], writes=[wu.r()], group="a0w")
                sl = slice(oc * T, (oc + 1) * T)
                ctx.tt(t1[:], X[:, sl], A[:], ALU.subtract, [Xr[oc], A.r()], [t1.r()])
                ctx.tt(t2[:], t1[:], B[:], ALU.mult, [t1.r(), B.r()], [t2.r()])
                for nb in range(4):
                    nsl = slice(nb * 512, (nb + 1) * 512)
                    ctx.mm_acc(ctx.pA[:, nsl],
                               [(wu[:, kc * 128:(kc + 1) * 128],
                                 pout[:, kc * T + nb * 512: kc * T + (nb + 1) * 512]) for kc in range(4)],
                               reads=[wu.r()] + poutr, wregion=("pA", nb), tag="a0up")
                    ctx.stt(xint[:, oc * T + nb * 512: oc * T + (nb + 1) * 512],
                            ctx.pA[:, nsl], upb[:, oc:oc + 1],
                            t2[:, nsl], ALU.add, ALU.add,
                            [("pA", nb), upb.r(), t2.r()], [xir[oc]])
                    ctx.stt(tu[:], ctx.pA[:, nsl], upb[:, oc:oc + 1], t2[:, nsl],
                            ALU.add, ALU.add, [("pA", nb), upb.r(), t2.r()], [tu.r()])
                    xsl = slice(oc * T + nb * 512, oc * T + (nb + 1) * 512)
                    ctx.tt(X[:, xsl], X[:, xsl], tu[:], ALU.add, [Xr[oc], tu.r()], [Xr[oc]])
        ctx.P.barrier()
        with ctx.scoped():
            n2w = load_pc(ctx, "a0_n2w_pc", 8)
            b1 = load_pc(ctx, "a0_b1_pc", 16)
            b2 = load_pc(ctx, "a0_b2_pc", 8)
            h2 = ctx.sb([128, 8 * T], BF16, "h2")
            h2r = [h2.r(c) for c in range(8)]
            with ctx.scoped():
                sqb = ctx.sb([128, T], BF16, "fsq0")
                fac = ctx.sb([128, T], F32, "fac0")
                for c in range(8):
                    sl = slice(c * T, (c + 1) * T)
                    ctx.actv(sqb[:], xint[:, sl], AF.Square, [xir[c]], [sqb.r()])
                    for nb in range(4):
                        osl = slice(nb * 512, (nb + 1) * 512)
                        ctx.mm(ctx.pB[:, osl], ctx.const["ones128"][:], sqb[:, osl],
                               [("const", "ones128"), sqb.r()], ("pB", nb),
                               start=(c == 0), stop=(c == 7), tag="a0S2b")
                pBr = [("pB", nb) for nb in range(4)]
                ctx.ts(fac[:], ctx.pB[:, 0:T], 1.0 / 1024, 1e-6, ALU.mult, ALU.add, pBr, [fac.r()])
                ctx.actv(fac[:], fac[:], AF.Ln, [fac.r()], [fac.r()])
                ctx.actv(fac[:], fac[:], AF.Exp, [fac.r()], [fac.r()], scale=-0.5)
                for c in range(8):
                    sl = slice(c * T, (c + 1) * T)
                    ctx.stt(h2[:, sl], xint[:, sl], n2w[:, c:c + 1], fac[:], ALU.mult, ALU.mult,
                            [xir[c], n2w.r(), fac.r()], [h2r[c]])
            ctx.P.barrier()
            gb = ctx.sb([128, 16 * 512], BF16, "gb")
            gbr = [gb.r(i) for i in range(16)]
            w1s = [ctx.sb([128, 1024], BF16, "w1s0") for _ in range(2)]
            w2s = [ctx.sb([128, 16 * 128], BF16, "w2s0") for _ in range(2)]
            for th in range(4):
                t0 = th * 512
                for i in range(16):
                    s = i % 2
                    ctx.dma("pool", w1s[s][:].rearrange("p (k m) -> p k m", m=128),
                            d["a0_ffn_w1"][:, i * 128:(i + 1) * 128].rearrange("(k p) m -> p k m", p=128),
                            reads=[], writes=[w1s[s].r()], group=f"a0w1{s}")
                    ctx.mm_acc(ctx.pA[:, 0:512],
                               [(w1s[s][:, kc * 128:(kc + 1) * 128],
                                 h2[:, kc * T + t0: kc * T + t0 + 512])
                                for kc in range(8)],
                               reads=[w1s[s].r()] + h2r, wregion=("pA", 0), tag="a0f1")
                    ctx.actv(gb[:, i * 512:(i + 1) * 512], ctx.pA[:, 0:512], AF.Gelu,
                             [("pA", 0), b1.r()], [gbr[i]],
                             bias=b1[:, i:i + 1])
                for oc in range(8):
                    s = oc % 2
                    ctx.dma("pool", w2s[s][:].rearrange("p (k m) -> p k m", m=128),
                            d["a0_ffn_w2"][:, oc * 128:(oc + 1) * 128].rearrange("(k p) m -> p k m", p=128),
                            reads=[], writes=[w2s[s].r()], group=f"a0w2{s}")
                    ctx.mm_acc(ctx.pB[:, 0:512],
                               [(w2s[s][:, kc * 128:(kc + 1) * 128],
                                 gb[:, kc * 512: (kc + 1) * 512])
                                for kc in range(16)],
                               reads=[w2s[s].r()] + gbr, wregion=("pB", 0), tag="a0f2")
                    xsl = slice(oc * T + t0, oc * T + t0 + 512)
                    ctx.stt(X[:, xsl], ctx.pB[:, 0:512], b2[:, oc:oc + 1],
                            X[:, xsl], ALU.add, ALU.add,
                            [("pB", 0), b2.r(), Xr[oc]], [Xr[oc]])


stage_l0attn.needs = ["a0_down_w", "a0_downb_pc", "a0_n1w_pc", "a0_n2w_pc",
                      "a0_qkv_w", "a0_qkvb_pc", "a0_bv", "a0_proj_w", "a0_projb_pc",
                      "a0_up_w", "a0_upb_pc", "a0_ffn_w1", "a0_b1_pc",
                      "a0_ffn_w2", "a0_b2_pc", "a0_bias"]



def _shifted_copy(ctx, dst, dst_regions, y, yr, dtype_note=""):
    for c in range(8):
        sl = slice(c * T, (c + 1) * T)
        eng = "ve" if c % 2 == 0 else "act"
        if c < 4:
            ctx.cp(eng, dst[:, sl], y[:, sl], [yr[c]], [dst_regions[c]])
        else:
            ctx.memset(dst[:, c * T: c * T + 1], 0.0, [dst_regions[c]], eng="pool")
            ctx.cp(eng, dst[:, c * T + 1:(c + 1) * T], y[:, c * T:(c + 1) * T - 1],
                   [yr[c]], [dst_regions[c]])


def stage_l1ffn(ctx):
    d = ctx.dram
    X = ctx.X
    Xr = [X.r(c) for c in range(8)]
    XrAll = Xr
    Xbf = X[:].bitcast(BF16)
    nc = ctx.nc
    with ctx.scoped():
        spname = ctx.nm("xspill")
        scratch_r = [X.r("s0"), X.r("s1")]
        sp = nc.dram_tensor(spname, [128, 8 * T], F32, kind="Internal").ap()
        ctx.dma("sp", sp[:, 0:4 * T], X[:, 0:4 * T], reads=Xr[0:4] + scratch_r,
                writes=[("dram", spname, 0)], group="xspill")
        ctx.dma("sp", sp[:, 4 * T:8 * T], X[:, 4 * T:8 * T], reads=Xr[4:8] + scratch_r,
                writes=[("dram", spname, 1)], group="xspill")
        y = ctx.sb([128, 8 * T], F32, "y")
        yr = [y.r(c) for c in range(8)]
        with ctx.scoped():
            f = ln_factors(ctx, X, Xr)
            A, B = f["A"], f["B"]
            t1 = ctx.sb([128, T], F32, "t1")
            t2 = ctx.sb([128, T], F32, "t2")
            for c in range(8):
                sl = slice(c * T, (c + 1) * T)
                eng = "ve"
                tl = t1 if c % 2 == 0 else t2
                ctx.tt(tl[:], X[:, sl], A[:], ALU.subtract, [Xr[c], A.r()], [tl.r()], eng=eng)
                ctx.tt(y[:, sl], tl[:], B[:], ALU.mult, [tl.r(), B.r()], [yr[c]], eng=eng)
        ctx.P.barrier()

        for li in range(6):
            _l1ffn_attn_sublayer(ctx, li, y, yr, Xbf, XrAll)
            ctx.P.barrier()
            _l1ffn_ffn_sublayer(ctx, li, y, yr, Xbf, XrAll)
            ctx.P.barrier()

        ctx.dma("sp", X[:, 0:4 * T], sp[:, 0:4 * T], reads=[("dram", spname, 0)],
                writes=Xr[0:4] + scratch_r, group="xrest")
        ctx.dma("sp", X[:, 4 * T:8 * T], sp[:, 4 * T:8 * T], reads=[("dram", spname, 1)],
                writes=Xr[4:8] + scratch_r, group="xrest")
        for c in range(8):
            sl = slice(c * T, (c + 1) * T)
            ctx.stt(X[:, sl], y[:, sl], 0.5, X[:, sl], ALU.mult, ALU.add,
                    [yr[c], Xr[c]], [Xr[c]])


def _l1ffn_attn_sublayer(ctx, li, y, yr, Xbf, XrAll):
    d = ctx.dram
    pAr = [("pA", nb) for nb in range(4)]
    pBr = [("pB", nb) for nb in range(4)]
    with ctx.scoped():
        tabs = {}
        for tn in ("rt_qcs", "rt_qss", "rt_kci", "rt_ksi"):
            tt_ = ctx.sb([128, T], BF16, tn)
            ctx.dma("sp", tt_[:], d[tn], reads=[], writes=[tt_.r()], group="small")
            tabs[tn] = tt_
        qs_col = ctx.sb([128, 1], F32, "qs")
        ks_col = ctx.sb([128, 1], F32, "ks")
        ctx.dma("sp", qs_col[:], d["f1_qs"][li].unsqueeze(1), reads=[], writes=[qs_col.r()], group="small")
        ctx.dma("sp", ks_col[:], d["f1_ks"][li].unsqueeze(1), reads=[], writes=[ks_col.r()], group="small")
        class _V:
            def __init__(self, ap):
                self.ap = ap
            def __getitem__(self, key):
                return self.ap[key]
        xs = _V(Xbf[:, 8 * T:16 * T])
        xsr = [ctx.X.r("s1")] * 8
        _shifted_copy(ctx, xs, xsr, y, yr)
        xsr = [ctx.X.r("s1")]
        aout = Xbf[:, 0:8 * T]
        aout_r = [ctx.X.r("s0")]

        qft = ctx.sb([128, T], BF16, "qft")
        kp = ctx.sb([128, 2304], BF16, "kp")
        vp = ctx.sb([128, 18 * 128], BF16, "vp")
        ctx.memset(kp[:, 0:256], 0.0, [kp.r("z")], eng="pool")
        ctx.memset(vp[:, 0:256], 0.0, [vp.r("z")], eng="pool")
        sq = ctx.sb([128, T], BF16, "sq")
        nrmq = ctx.sb([128, T], BF16, "nrmq")
        nrmk = ctx.sb([128, T], BF16, "nrmk")
        qhb = ctx.sb([128, T], BF16, "qhb")
        qraw = ctx.sb([128, T], BF16, "qraw")
        kraw = ctx.sb([128, T], BF16, "kraw")
        tmp1 = sq
        eps24 = ctx.sb([128, 1], F32, "eps24")
        ctx.memset(eps24[:], 1e-24, [eps24.r()], eng="pool")
        NRING = 5
        NRT = 3
        ptl = [ctx.sb([128, 384], BF16, "ptl") for _ in range(NRING)]
        ptlT = [ctx.sb([128, 384], BF16, "ptlT") for _ in range(NRT)]
        mx = [ctx.sb([128, 1], F32, "mx") for _ in range(NRING)]
        den = [ctx.sb([128, 1], F32, "den") for _ in range(NRING)]
        wsl = [ctx.sb([128, 3 * 1024], BF16, "wq1") for _ in range(2)]

        def l2norm_phase(raw, raw_r, nrm, psum, pname):
            pr4 = [(pname, nb) for nb in range(4)]
            ctx.actv(sq[:], raw[:], AF.Square, [raw_r], [sq.r()])
            for nb in range(4):
                ctx.mm(psum[:, nb * 512:(nb + 1) * 512], ctx.const["block64"][:],
                       sq[:, nb * 512:(nb + 1) * 512], [("const", "block64"), sq.r()],
                       (pname, nb), tag="nrm")
            ctx.actv(nrm[:], psum[:, 0:T], AF.Ln, pr4 + [eps24.r()], [nrm.r()],
                     bias=eps24[:])
            ctx.actv(nrm[:], nrm[:], AF.Exp, [nrm.r()], [nrm.r()], scale=-0.5)

        def l2rot_phase(raw, raw_r, nrm, gain_col, gain_r, cs, ss, outt, wout):
            pBr4 = [("pB", nb) for nb in range(4)]
            ctx.stt(qhb[:], raw[:], gain_col[:], nrm[:], ALU.mult, ALU.mult,
                    [raw_r, gain_r, nrm.r()], [qhb.r()])
            for nb in range(4):
                ctx.mm(ctx.pB[:, nb * 512:(nb + 1) * 512], ctx.const["rotperm"][:],
                       qhb[:, nb * 512:(nb + 1) * 512], [("const", "rotperm"), qhb.r()],
                       ("pB", nb), tag="rot")
            ctx.cp("act", raw[:], ctx.pB[:, 0:T], pBr4, [raw_r])
            ctx.tt(tmp1[:], qhb[:], cs[:], ALU.mult, [qhb.r(), cs.r()], [tmp1.r()])
            ctx.tt(nrm[:], raw[:], ss[:], ALU.mult, [raw_r, ss.r()], [nrm.r()])
            ctx.tt(outt, tmp1[:], nrm[:], ALU.add, [tmp1.r(), nrm.r()], wout)

        for hc in range(8):
            s = hc % 2
            for part in range(3):
                ctx.dma("pool", wsl[s][:, part * 1024:(part + 1) * 1024].rearrange("p (k m) -> p k m", m=128),
                        d["f1_wqkv"][li, part * 8 + hc],
                        reads=[], writes=[wsl[s].r()], group=f"q1{s}")
            wr = [wsl[s].r()]
            for nb in range(4):
                nsl = slice(nb * 512, (nb + 1) * 512)
                ctx.mm_acc(ctx.pA[:, nsl],
                           [(wsl[s][:, kc * 128:(kc + 1) * 128],
                             xs[:, kc * T + nb * 512: kc * T + (nb + 1) * 512]) for kc in range(8)],
                           reads=wr + xsr, wregion=("pA", nb), tag="q1")
            ctx.cp("ve", qraw[:], ctx.pA[:, 0:T], pAr, [qraw.r()])
            for nb in range(4):
                nsl = slice(nb * 512, (nb + 1) * 512)
                ctx.mm_acc(ctx.pA[:, nsl],
                           [(wsl[s][:, 1024 + kc * 128: 1024 + (kc + 1) * 128],
                             xs[:, kc * T + nb * 512: kc * T + (nb + 1) * 512]) for kc in range(8)],
                           reads=wr + xsr, wregion=("pA", nb), tag="k1")
            l2norm_phase(qraw, qraw.r(), nrmq, ctx.pB, "pB")
            ctx.cp("ve", kraw[:], ctx.pA[:, 0:T], pAr, [kraw.r()])
            ctx.actv(sq[:], kraw[:], AF.Square, [kraw.r()], [sq.r()])
            for tc in range(16):
                bank = tc % 4
                psl = slice(bank * 512, bank * 512 + 128)
                ctx.mm_acc(ctx.pA[:, psl],
                           [(xs[:, kc * T + tc * 128: kc * T + (tc + 1) * 128],
                             wsl[s][:, 2048 + kc * 128: 2048 + (kc + 1) * 128]) for kc in range(8)],
                           reads=wr + xsr, wregion=("pA", bank), tag="v1")
                ctx.cp("act" if tc % 2 else "ve", vp[:, (tc + 2) * 128:(tc + 3) * 128],
                       ctx.pA[:, psl], [("pA", bank)], [vp.r(tc)])
            pr4 = [("pA", nb) for nb in range(4)]
            for nb in range(4):
                ctx.mm(ctx.pA[:, nb * 512:(nb + 1) * 512], ctx.const["block64"][:],
                       sq[:, nb * 512:(nb + 1) * 512], [("const", "block64"), sq.r()],
                       ("pA", nb), tag="nrm")
            ctx.actv(nrmk[:], ctx.pA[:, 0:T], AF.Ln, pr4 + [eps24.r()], [nrmk.r()],
                     bias=eps24[:])
            ctx.actv(nrmk[:], nrmk[:], AF.Exp, [nrmk.r()], [nrmk.r()], scale=-0.5)
            l2rot_phase(qraw, qraw.r(), nrmq, qs_col, qs_col.r(),
                        tabs["rt_qcs"], tabs["rt_qss"], qft[:], [qft.r()])
            l2rot_phase(kraw, kraw.r(), nrmk, ks_col, ks_col.r(),
                        tabs["rt_kci"], tabs["rt_ksi"], kp[:, 256:256 + T], [kp.r("k")])
            vpr = [vp.r(tc) for tc in range(16)] + [vp.r("z")]
            groups = [(wi, qc2, j) for wi in range(8) for qc2 in range(2) for j in range(2)]
            LOOK = 4

            def front(g):
                wi, qc2, j = groups[g]
                b0 = 64 * j
                q0 = wi * 256 + qc2 * 128
                bank = g % 4
                ssl = slice(bank * 512, bank * 512 + 384)
                rg = g % NRING
                ctx.mm(ctx.pB[:, ssl], qft[b0:b0 + 64, q0:q0 + 128],
                       kp[b0:b0 + 64, q0: q0 + 384],
                       [qft.r(), kp.r("k"), kp.r("z")], ("pB", bank),
                       start=True, stop=False, tag="sc1")
                mvar = f"lmask{1 if wi == 0 else 0}{qc2}"
                ctx.mm(ctx.pB[:, ssl], ctx.const["ident"][:], ctx.const[mvar][:],
                       [("const", "ident"), ("const", mvar)], ("pB", bank),
                       start=False, stop=True, tag="msk")
                ctx.red(mx[rg][:], ctx.pB[:, ssl], ALU.max, [("pB", bank)], [mx[rg].r()],
                        negate=True)
                ctx.actv(ptl[rg][:], ctx.pB[:, ssl], AF.Exp, [("pB", bank), mx[rg].r()],
                         [ptl[rg].r(), den[rg].r()], bias=mx[rg][:], accum=den[rg][:])
                ctx.recip(den[rg][:], den[rg][:], [den[rg].r()], [den[rg].r()])
                ctx.ts(ptl[rg][:], ptl[rg][:], den[rg][:], None, ALU.mult, None,
                       [ptl[rg].r(), den[rg].r()], [ptl[rg].r()])

            def back_tr(g):
                rg = g % NRING
                rt = g % NRT
                tb = g % 2
                for jc in range(3):
                    psl = slice(tb * 512 + jc * 128, tb * 512 + (jc + 1) * 128)
                    ctx.mm(ctx.pA[:, psl], ptl[rg][:, jc * 128:(jc + 1) * 128],
                           ctx.const["ident"][:], [ptl[rg].r(), ("const", "ident")],
                           ("pA", tb), tag="pT1")
                ctx.cp("act" if g % 2 == 0 else "ve", ptlT[rt][:],
                       ctx.pA[:, tb * 512: tb * 512 + 384],
                       [("pA", tb)], [ptlT[rt].r(jc) for jc in range(3)])

            def back_av(g):
                wi, qc2, j = groups[g]
                b0 = 64 * j
                rt = g % NRT
                avbank = 2 + ((g // 2) % 2)
                av_sl2 = slice(avbank * 512, avbank * 512 + 128)
                ctx.mm_acc(ctx.pA[b0:b0 + 64, av_sl2],
                           [(vp[:, (2 * wi + qc2 + jc) * 128 + b0:
                                 (2 * wi + qc2 + jc) * 128 + b0 + 64],
                             ptlT[rt][:, jc * 128:(jc + 1) * 128]) for jc in range(3)],
                           reads=vpr + [ptlT[rt].r(jc) for jc in range(3)],
                           wregion=("pA", avbank), tag="av1", tile_position=(0, b0))
                if j == 1:
                    ctx.cp("act", aout[:, hc * T + wi * 256 + qc2 * 128:
                                       hc * T + wi * 256 + (qc2 + 1) * 128],
                           ctx.pA[:, av_sl2], [("pA", avbank)], aout_r)

            NG = len(groups)
            for gi in range(NG + LOOK + 2):
                if gi >= LOOK + 2:
                    back_av(gi - LOOK - 2)
                if LOOK <= gi < NG + LOOK:
                    back_tr(gi - LOOK)
                if gi < NG:
                    front(gi)
        for oc in range(8):
            s = oc % 2
            ps = ctx.pA if oc % 2 == 0 else ctx.pB
            pn = "pA" if oc % 2 == 0 else "pB"
            ctx.dma("pool", wsl[s][:, 0:1024].rearrange("p (k m) -> p k m", m=128),
                    d["f1_wo"][li, oc],
                    reads=[], writes=[wsl[s].r()], group=f"q1{s}")
            for nb in range(4):
                nsl = slice(nb * 512, (nb + 1) * 512)
                ctx.mm_acc(ps[:, nsl],
                           [(wsl[s][:, kc * 128:(kc + 1) * 128],
                             aout[:, kc * T + nb * 512: kc * T + (nb + 1) * 512]) for kc in range(8)],
                           reads=[wsl[s].r()] + aout_r, wregion=(pn, nb), tag="wo1")
                ysl = slice(oc * T + nb * 512, oc * T + (nb + 1) * 512)
                ctx.tt(y[:, ysl], y[:, ysl], ps[:, nsl], ALU.add,
                       [yr[oc], (pn, nb)], [yr[oc]])


def _l1ffn_ffn_sublayer(ctx, li, y, yr, Xbf, XrAll):
    d = ctx.dram
    DI2 = 2816
    with ctx.scoped():
        g_pc = load_pc(ctx, "f1_lng", 8 * 6)
        b_pc = load_pc(ctx, "f1_lnb", 8 * 6)
        b1_pc = load_pc(ctx, "f1_b1", 44 * 6)
        b2_pc = load_pc(ctx, "f1_b2", 8 * 6)
        xn2 = ctx.sb([128, 8 * T], BF16, "xn2")
        xnr = [xn2.r(c) for c in range(8)]
        onesr = ("const", "ones128")
        ones = ctx.const["ones128"]
        with ctx.scoped():
            sqb = ctx.sb([128, T], BF16, "fsq")
            A = ctx.sb([128, T], F32, "fA")
            B = ctx.sb([128, T], F32, "fB")
            tmp = ctx.sb([128, T], F32, "ftmp")
            _shifted_copy(ctx, xn2, xnr, y, yr)
            for c in range(8):
                sl = slice(c * T, (c + 1) * T)
                ctx.actv(sqb[:], xn2[:, sl], AF.Square, [xnr[c]], [sqb.r()])
                for nb in range(4):
                    osl = slice(nb * 512, (nb + 1) * 512)
                    ctx.mm(ctx.pA[:, osl], ones[:], xn2[:, c * T + nb * 512: c * T + (nb + 1) * 512],
                           [onesr, xnr[c]], ("pA", nb), start=(c == 0), stop=(c == 7), tag="fS1")
                    ctx.mm(ctx.pB[:, osl], ones[:], sqb[:, osl], [onesr, sqb.r()],
                           ("pB", nb), start=(c == 0), stop=(c == 7), tag="fS2")
            pAr = [("pA", nb) for nb in range(4)]
            pBr = [("pB", nb) for nb in range(4)]
            ctx.ts(A[:], ctx.pA[:, 0:T], 1.0 / 1024, None, ALU.mult, None, pAr, [A.r()])
            ctx.tt(tmp[:], A[:], A[:], ALU.mult, [A.r()], [tmp.r()])
            ctx.ts(B[:], ctx.pB[:, 0:T], 1.0 / 1024, 1e-5, ALU.mult, ALU.add, pBr, [B.r()])
            ctx.tt(tmp[:], B[:], tmp[:], ALU.subtract, [B.r(), tmp.r()], [tmp.r()])
            ctx.actv(B[:], tmp[:], AF.Ln, [tmp.r()], [B.r()])
            ctx.actv(B[:], B[:], AF.Exp, [B.r()], [B.r()], scale=-0.5)
            tmp2 = ctx.sb([128, T], F32, "ftmp2")
            for c in range(8):
                sl = slice(c * T, (c + 1) * T)
                eng = "ve"
                tl = tmp if c % 2 == 0 else tmp2
                ctx.tt(tl[:], xn2[:, sl], A[:], ALU.subtract, [xnr[c], A.r()], [tl.r()],
                       eng=eng)
                ctx.tt(tl[:], tl[:], B[:], ALU.mult, [tl.r(), B.r()], [tl.r()], eng=eng)
                ctx.ts(xn2[:, sl], tl[:], g_pc[:, li * 8 + c: li * 8 + c + 1],
                       b_pc[:, li * 8 + c: li * 8 + c + 1],
                       ALU.mult, ALU.add, [tl.r(), g_pc.r(), b_pc.r()], [xnr[c]], eng="ve")
        ctx.P.barrier()
        hbuf = Xbf
        gelb2 = [ctx.sb([128, 1024], BF16, "gelb") for _ in range(2)]
        w1s = [ctx.sb([128, 2 * 1024], BF16, "w1s") for _ in range(4)]
        w2s = [ctx.sb([128, 22 * 128], BF16, "w2s") for _ in range(3)]
        for th in range(2):
            t0 = th * 1024
            for i in range(22):
                s = i % 4
                ctx.dma("pool", w1s[s][:, 0:1024].rearrange("p (k m) -> p k m", m=128),
                        d["f1_w1"][li, i],
                        reads=[], writes=[w1s[s].r()], group=f"w1{s}")
                ctx.dma("pool", w1s[s][:, 1024:2048].rearrange("p (k m) -> p k m", m=128),
                        d["f1_w1"][li, 22 + i],
                        reads=[], writes=[w1s[s].r()], group=f"w1{s}")
                bs = 2 * (i % 2)
                for nb in range(2):
                    nsl = slice((bs + nb) * 512, (bs + nb + 1) * 512)
                    ctx.mm_acc(ctx.pA[:, nsl],
                               [(w1s[s][:, kc * 128:(kc + 1) * 128],
                                 xn2[:, kc * T + t0 + nb * 512: kc * T + t0 + (nb + 1) * 512])
                                for kc in range(8)],
                               reads=[w1s[s].r()] + xnr, wregion=("pA", bs + nb), tag="w1a")
                    ctx.mm_acc(ctx.pB[:, nsl],
                               [(w1s[s][:, 1024 + kc * 128: 1024 + (kc + 1) * 128],
                                 xn2[:, kc * T + t0 + nb * 512: kc * T + t0 + (nb + 1) * 512])
                                for kc in range(8)],
                               reads=[w1s[s].r()] + xnr, wregion=("pB", bs + nb), tag="w1g")
                gelb = gelb2[i % 2]
                ctx.actv(gelb[:], ctx.pB[:, bs * 512: bs * 512 + 1024], AF.Gelu,
                         [("pB", bs), ("pB", bs + 1), b1_pc.r()],
                         [gelb.r()], bias=b1_pc[:, li * 44 + 22 + i: li * 44 + 23 + i])
                ctx.stt(hbuf[:, i * 1024:(i + 1) * 1024], ctx.pA[:, bs * 512: bs * 512 + 1024],
                        b1_pc[:, li * 44 + i: li * 44 + i + 1], gelb[:], ALU.add, ALU.mult,
                        [("pA", bs), ("pA", bs + 1), b1_pc.r(), gelb.r()],
                        [ctx.X.r("s0"), ctx.X.r("s1")])
            for oc in range(8):
                s = oc % 3
                bs = 2 * (oc % 2)
                ctx.dma("pool", w2s[s][:].rearrange("p (k m) -> p k m", m=128),
                        d["f1_w2"][li, oc],
                        reads=[], writes=[w2s[s].r()], group=f"w2{s}")
                for nb in range(2):
                    nsl = slice((bs + nb) * 512, (bs + nb + 1) * 512)
                    ctx.mm_acc(ctx.pA[:, nsl],
                               [(w2s[s][:, kc * 128:(kc + 1) * 128],
                                 hbuf[:, kc * 1024 + nb * 512: kc * 1024 + (nb + 1) * 512])
                                for kc in range(22)],
                               reads=[w2s[s].r(), ctx.X.r("s0"), ctx.X.r("s1")],
                               wregion=("pA", bs + nb), tag="w2")
                    ysl = slice(oc * T + t0 + nb * 512, oc * T + t0 + (nb + 1) * 512)
                    ctx.stt(y[:, ysl], ctx.pA[:, nsl], b2_pc[:, li * 8 + oc: li * 8 + oc + 1],
                            y[:, ysl], ALU.add, ALU.add,
                            [("pA", bs + nb), b2_pc.r(), yr[oc]], [yr[oc]])


stage_l1ffn.needs = ["f1_wqkv", "f1_qs", "f1_ks", "f1_wo", "f1_lng", "f1_lnb",
                     "f1_w1", "f1_b1", "f1_w2", "f1_b2",
                     "rt_qcs", "rt_qss", "rt_kci", "rt_ksi"]



def build_program(stages, final_out=True):
    nc = bass.Bass("TRN2", target_bir_lowering=False)
    dram = {}
    dram["xT"] = nc.dram_tensor("xT", [D, T], F32, kind="ExternalInput").ap()
    need = set()
    for s in stages:
        need.update(s.needs)
    for n in sorted(need):
        shp, dt = INPUT_SPECS[n]
        dram[n] = nc.dram_tensor(n, list(shp), dt, kind="ExternalInput").ap()
    if final_out:
        out_ap = nc.dram_tensor("out", [T, D], F32, kind="ExternalOutput").ap()
    else:
        out_ap = nc.dram_tensor("xout", [D, T], F32, kind="ExternalOutput").ap()

    P = Prog(nc)
    ctx = Ctx(nc, P, dram, host_constants())
    xin = dram["xT"].rearrange("(c p) t -> p c t", p=128)
    x3 = ctx.X[:].rearrange("p (c t) -> p c t", t=T)
    ctx.dma("sp", x3[:, 0:4], xin[:, 0:4], reads=[],
            writes=[ctx.X.r(c) for c in range(4)], group="xload")
    ctx.dma("sp", x3[:, 4:8], xin[:, 4:8], reads=[],
            writes=[ctx.X.r(c) for c in range(4, 8)], group="xload")
    for s in stages:
        s(ctx)
        P.barrier()
    if final_out:
        with ctx.scoped():
            ot = ctx.sb([128, 1024], F32, "ot")
            for tc in range(16):
                for c in range(8):
                    ctx.mm(ctx.pA[:, 0:128] if c % 2 == 0 else ctx.pA[:, 512:640],
                           ctx.X[:, c * T + tc * 128: c * T + (tc + 1) * 128],
                           ctx.const["identf"][:],
                           [ctx.X.r(c), ("const", "identf")], ("pA", c % 2), tag="outT")
                    ctx.cp("ve", ot[:, c * 128:(c + 1) * 128],
                           ctx.pA[:, 0:128] if c % 2 == 0 else ctx.pA[:, 512:640],
                           [("pA", c % 2)], [ot.r(c)])
                ctx.dma("sp", out_ap[tc * 128:(tc + 1) * 128, :], ot[:],
                        reads=[ot.r(c) for c in range(8)], writes=[("dram", "out", tc)],
                        group="xstore")
    else:
        xo = out_ap.rearrange("(c p) t -> p c t", p=128)
        x3 = ctx.X[:].rearrange("p (c t) -> p c t", t=T)
        ctx.dma("sp", xo[:, 0:4], x3[:, 0:4],
                reads=[ctx.X.r(c) for c in range(4)], writes=[("dram", "xo0")], group="xstore")
        ctx.dma("sp", xo[:, 4:8], x3[:, 4:8],
                reads=[ctx.X.r(c) for c in range(4, 8)], writes=[("dram", "xo1")], group="xstore")
    P.op("sp", lambda e: e.nop(),
         reads=[("dram", "out", tc) for tc in range(16)] if final_out
         else [("dram", "xo0"), ("dram", "xo1")],
         writes=[], tag="final_wait")
    P.finalize()
    return nc



INPUT_SPECS = {
    "moe_scale_pc": ((128, 8), F32),
    "moe_routerT": ((D, 8), F32),
    "moe_upT": ((8, 32, 128, 8, 128), F32),
    "moe_downT": ((8, 8, 128, 16, 128), F32),
    "a1_qkvT": ((24, 128, 8, 128), F32),
    "a1_bqkv_pc": ((128, 24), F32),
    "a1_bv_row": ((1, 1024), F32),
    "a1_woT": ((8, 128, 8, 128), F32),
    "a1_bo_pc": ((128, 8), F32),
    "f1_wqkv": ((6, 24, 128, 8, 128), F32),
    "f1_qs": ((6, 128), F32),
    "f1_ks": ((6, 128), F32),
    "f1_wo": ((6, 8, 128, 8, 128), F32),
    "f1_lng": ((128, 48), F32),
    "f1_lnb": ((128, 48), F32),
    "f1_w1": ((6, 44, 128, 8, 128), F32),
    "f1_b1": ((128, 264), F32),
    "f1_w2": ((6, 8, 128, 22, 128), F32),
    "f1_b2": ((128, 48), F32),
    "rt_qcs": ((128, T), BF16),
    "rt_qss": ((128, T), BF16),
    "rt_kci": ((128, T), BF16),
    "rt_ksi": ((128, T), BF16),
    "a0_down_w": ((D, 512), F32),
    "a0_downb_pc": ((128, 4), F32),
    "a0_n1w_pc": ((128, 4), F32),
    "a0_n2w_pc": ((128, 8), F32),
    "a0_qkv_w": ((2, 256, 768), F32),
    "a0_qkvb_pc": ((128, 12), F32),
    "a0_bv": ((2, 1, 256), F32),
    "a0_proj_w": ((2, 256, 256), F32),
    "a0_projb_pc": ((128, 4), F32),
    "a0_up_w": ((512, D), F32),
    "a0_upb_pc": ((128, 8), F32),
    "a0_ffn_w1": ((D, 2048), F32),
    "a0_b1_pc": ((128, 16), F32),
    "a0_ffn_w2": ((2048, D), F32),
    "a0_b2_pc": ((128, 8), F32),
    "a0_bias": ((2, 8, 16, 128, 1024), BF16),
}

L0_GEOM = ((10, 5, 2050), (20, 10, 2060))


def _l0_band_starts(qc):
    s0 = min(max((qc - 1) * 128, 0), L0_L - 512)
    s1 = 1664 if qc != 15 else 0
    return s0, s1


def pc(v):
    v = _f32(v)
    return np.ascontiguousarray(v.reshape(-1, 128).T)


def lin_lhsT(W):
    W = np.asarray(W, np.float32)
    din, dout = W.shape
    a = W.reshape(din // 128, 128, dout // 128, 128)
    return np.ascontiguousarray(a.transpose(2, 1, 0, 3))


def host_prep(inputs):
    s = {}
    s["moe_scale_pc"] = pc(inputs["L0_ffn_norm_scale"])
    s["moe_routerT"] = _f32(np.ascontiguousarray(np.asarray(inputs["L0_ffn_w_router"]).T))
    s["moe_upT"] = np.stack([lin_lhsT(np.asarray(inputs["L0_ffn_w_up"])[e].T)
                             for e in range(8)])
    s["moe_downT"] = np.stack([lin_lhsT(np.asarray(inputs["L0_ffn_w_down"])[e].T)
                               for e in range(8)])
    bqkv = np.asarray(inputs["L1_attn_b_qkv"])
    s["a1_qkvT"] = lin_lhsT(np.asarray(inputs["L1_attn_w_qkv"]).T)
    s["a1_bqkv_pc"] = pc(bqkv)
    s["a1_bv_row"] = _f32(bqkv[2048:3072].reshape(1, 1024))
    s["a1_woT"] = lin_lhsT(np.asarray(inputs["L1_attn_w_o"]).T)
    s["a1_bo_pc"] = pc(inputs["L1_attn_b_o"])

    s["f1_wqkv"] = np.stack([lin_lhsT(np.asarray(inputs["L1_ffn_Wqkv"])[i])
                             for i in range(6)])
    s["f1_qs"] = _f32(np.tile(np.asarray(inputs["L1_ffn_q_scale"]), (1, 2)))
    s["f1_ks"] = _f32(np.tile(np.asarray(inputs["L1_ffn_k_scale"]), (1, 2)))
    s["f1_wo"] = np.stack([lin_lhsT(np.asarray(inputs["L1_ffn_Wo"])[i])
                           for i in range(6)])
    s["f1_lng"] = np.concatenate([pc(np.asarray(inputs["L1_ffn_ln_g"])[i]) for i in range(6)], axis=1)
    s["f1_lnb"] = np.concatenate([pc(np.asarray(inputs["L1_ffn_ln_b"])[i]) for i in range(6)], axis=1)
    W1 = np.asarray(inputs["L1_ffn_W1"], dtype=np.float32)
    W1p = np.zeros((6, D, 5632), np.float32)
    W1p[:, :, 0:2730] = W1[:, :, 0:2730]
    W1p[:, :, 2816:2816 + 2730] = W1[:, :, 2730:5460]
    s["f1_w1"] = np.stack([lin_lhsT(W1p[i]) for i in range(6)])
    b1 = np.asarray(inputs["L1_ffn_b1"], dtype=np.float32)
    b1p = np.zeros((6, 5632), np.float32)
    b1p[:, 0:2730] = b1[:, 0:2730]
    b1p[:, 2816:2816 + 2730] = b1[:, 2730:5460]
    s["f1_b1"] = np.concatenate([pc(b1p[i]) for i in range(6)], axis=1)
    W2 = np.asarray(inputs["L1_ffn_W2"], dtype=np.float32)
    W2p = np.zeros((6, 2816, D), np.float32)
    W2p[:, 0:2730, :] = W2
    s["f1_w2"] = np.stack([lin_lhsT(W2p[i]) for i in range(6)])
    s["f1_b2"] = np.concatenate([pc(np.asarray(inputs["L1_ffn_b2"])[i]) for i in range(6)], axis=1)

    DH, WIN = 64, 256
    half = np.arange(0, DH, 2, dtype=np.float64)
    inv_freq = 1.0 / 10000.0 ** (half / DH)
    t = np.arange(T, dtype=np.float64)
    freqs = t[:, None] * inv_freq[None, :]
    freqs = np.concatenate([freqs, freqs], axis=1)
    base_scale = (half + 0.4 * DH) / (1.4 * DH)
    power = (t - T // 2) / (WIN // 2)
    scale = base_scale[None, :] ** power[:, None]
    scale = np.concatenate([scale, scale], axis=1)
    cs, sn = np.cos(freqs), np.sin(freqs)
    def featmaj(a):
        return _bf(np.tile(a.T.astype(np.float32), (2, 1)))
    s["rt_qcs"] = featmaj(8.0 * cs * scale)
    s["rt_qss"] = featmaj(8.0 * sn * scale)
    s["rt_kci"] = featmaj(cs / scale)
    s["rt_ksi"] = featmaj(sn / scale)

    s["a0_down_w"] = _f32(inputs["L0_attn_down_w"])
    s["a0_downb_pc"] = pc(inputs["L0_attn_down_b"])
    s["a0_n1w_pc"] = pc(inputs["L0_attn_norm1_w"])
    s["a0_n2w_pc"] = pc(inputs["L0_attn_norm2_w"])
    s["a0_qkv_w"] = _f32(np.stack([np.asarray(inputs["L0_attn_qkv1_w"]),
                                   np.asarray(inputs["L0_attn_qkv2_w"])]))
    bq1 = np.asarray(inputs["L0_attn_qkv1_b"], np.float32)
    bq2 = np.asarray(inputs["L0_attn_qkv2_b"], np.float32)
    s["a0_qkvb_pc"] = np.concatenate([pc(bq1[0:512]), pc(bq2[0:512])], axis=1)
    qkvb = np.zeros((128, 12), np.float32)
    for br, bq in enumerate((bq1, bq2)):
        for part in range(2):
            seg = bq[part * 256:(part + 1) * 256]
            qkvb[:, br * 6 + part * 2 + 0] = seg[0:128]
            qkvb[:, br * 6 + part * 2 + 1] = seg[128:256]
    s["a0_qkvb_pc"] = qkvb
    s["a0_bv"] = np.stack([bq1[512:768].reshape(1, 256), bq2[512:768].reshape(1, 256)])
    s["a0_proj_w"] = _f32(np.stack([np.asarray(inputs["L0_attn_proj1_w"]),
                                    np.asarray(inputs["L0_attn_proj2_w"])]))
    pb1 = np.asarray(inputs["L0_attn_proj1_b"], np.float32)
    pb2 = np.asarray(inputs["L0_attn_proj2_b"], np.float32)
    projb = np.zeros((128, 4), np.float32)
    projb[:, 0], projb[:, 1] = pb1[0:128], pb1[128:256]
    projb[:, 2], projb[:, 3] = pb2[0:128], pb2[128:256]
    s["a0_projb_pc"] = projb
    s["a0_up_w"] = _f32(inputs["L0_attn_up_w"])
    s["a0_upb_pc"] = pc(inputs["L0_attn_up_b"])
    s["a0_ffn_w1"] = _f32(inputs["L0_attn_ffn_w1"])
    s["a0_b1_pc"] = pc(inputs["L0_attn_ffn_b1"])
    s["a0_ffn_w2"] = _f32(inputs["L0_attn_ffn_w2"])
    s["a0_b2_pc"] = pc(inputs["L0_attn_ffn_b2"])

    rpbs = (np.asarray(inputs["L0_attn_rpb1"], np.float32),
            np.asarray(inputs["L0_attn_rpb2"], np.float32))
    bias = np.full((2, 8, 16, 128, 1024), -1e38, np.float32)
    for br, (w, shift, Tp) in enumerate(L0_GEOM):
        rpb = rpbs[br]
        for qc in range(16):
            s0, s1 = _l0_band_starts(qc)
            q = qc * 128 + np.arange(128)
            i = (q - shift) % Tp
            for bi, st in enumerate((s0, s1)):
                if bi == 1 and qc not in (0, 15):
                    continue
                k = st + np.arange(512)
                valid = k < Tp
                jj = (k - shift) % Tp
                same = (i[:, None] // w) == (jj[None, :] // w)
                allowed = same & valid[None, :]
                rel = np.clip(i[:, None] - jj[None, :] + w - 1, 0, 2 * w - 2)
                for h in range(8):
                    bias[br, h, qc, :, bi * 512:(bi + 1) * 512] = np.where(
                        allowed, rpb[rel, h], -1e38)
    s["a0_bias"] = _bf(bias)
    return s



def _bf(x):
    return np.ascontiguousarray(x, dtype=BF)

def _f32(x):
    return np.ascontiguousarray(x, dtype=np.float32)


def host_constants():
    c = {}
    c["ident"] = _bf(np.eye(128, dtype=np.float32))
    c["identf"] = _f32(np.eye(128, dtype=np.float32))
    c["onescol_f"] = _f32(np.ones((1, 128), np.float32))
    c["ones128"] = _bf(np.ones((128, 128), np.float32))
    b = np.zeros((128, 128), np.float32)
    b[:64, :64] = 1.0
    b[64:, 64:] = 1.0
    c["block64"] = _bf(b)
    r = np.zeros((128, 128), np.float32)
    for m in range(128):
        if m % 64 < 32:
            r[m + 32, m] = -1.0
        else:
            r[m - 32, m] = 1.0
    c["rotperm"] = _bf(r)
    for edge in range(2):
        for qc2 in range(2):
            m = np.zeros((128, 384), np.float32)
            for p in range(128):
                for n in range(384):
                    dqk = p - n + 256
                    ok = 0 <= dqk <= 256
                    if edge and n < 256 - qc2 * 128:
                        ok = False
                    if not ok:
                        m[p, n] = -1e38
            c[f"lmask{edge}{qc2}"] = _bf(m)
    for br, (w, shift, Tp) in enumerate(L0_GEOM):
        mseg = np.zeros(Tp, np.int32)
        mseg[Tp - w: Tp - shift] = 1
        mseg[Tp - shift:] = 2
        for qc in (0, 15):
            if br == 1 and qc == 15:
                continue
            s0, s1 = _l0_band_starts(qc)
            am = np.zeros((128, 1024), np.float32)
            q = qc * 128 + np.arange(128)
            i = (q - shift) % Tp
            for bi, st in enumerate((s0, s1)):
                k = st + np.arange(512)
                valid = k < Tp
                jj = (k - shift) % Tp
                diff = (mseg[i][:, None] != mseg[jj][None, :]) & valid[None, :]
                am[:, bi * 512:(bi + 1) * 512] = np.where(diff, -100.0, 0.0)
            c[f"amask{br}q{qc}"] = _bf(am)
    return c



STAGE_LIST = [stage_moe, stage_l0attn, stage_moe, stage_l1ffn, stage_l1attn, stage_l1ffn]


def run_stages(x, staged, stage_groups):
    cur = [np.ascontiguousarray(np.asarray(x)[b].T, dtype=np.float32) for b in range(NC)]
    for gi, group in enumerate(stage_groups):
        final = gi == len(stage_groups) - 1
        nc = build_program(group, final_out=final)
        need = set()
        for s in group:
            need.update(s.needs)
        shared = {n: staged[n] for n in need}
        in_maps = [dict(shared, xT=cur[b]) for b in range(NC)]
        res = run_bass_kernel_spmd(nc, in_maps, core_ids=list(range(NC)))
        if final:
            return np.stack([res.results[b]["out"] for b in range(NC)], axis=0)
        cur = [res.results[b]["xout"] for b in range(NC)]
    raise AssertionError


FUSED = True


def kernel(**inputs):
    staged = host_prep(inputs)
    x = np.asarray(inputs["L0_attn_x"], dtype=np.float32)
    groups = [list(STAGE_LIST)] if FUSED else [[s] for s in STAGE_LIST]
    return run_stages(x, staged, groups)
